# version 05
# baseline (speedup 1.0000x reference)
import numpy as np
from contextlib import ExitStack

import concourse.bass as bass
import concourse.mybir as mybir
from concourse.bass_utils import run_bass_kernel_spmd

F32 = mybir.dt.float32
BF16 = mybir.dt.bfloat16
AF = mybir.ActivationFunctionType
ALU = mybir.AluOpType
AX = mybir.AxisListType

B, L, D = 4, 4096, 2048
NCORE = 8
T = 2048
NT = 16
NH = 32
HC = 16
E = 8
IC = 32
HQ = 4
TP = 512
NPASS = 4
TCP = 4
CH = 16
UNIT = 4096
GU_UNITS = 32
NUNIT_E = 48
POOL = 5
EPS = 1e-5
INV_D = 1.0 / D
TWO_PI = 6.283185307179586
PI = 3.141592653589793
ECOLS = 3 * 65536


class Sem:
    def __init__(self, h):
        self.h = h
        self.n = 0

    def inc(self, k=1):
        self.n += k
        return self.n


class Prog:
    ENGS = ("tensor", "vector", "scalar", "gpsimd", "sync")

    def __init__(self):
        self.ops = {e: [] for e in self.ENGS}

    def em(self, eng, fn):
        self.ops[eng].append(fn)

    def wait(self, eng, sem, val):
        if val > 0:
            self.ops[eng].append(lambda e, _s=sem.h, _v=val: e.wait_ge(_s, _v))

    def replay(self, blk):
        for name in self.ENGS:
            if not self.ops[name]:
                continue
            ops = self.ops[name]

            def body(eng, _ops=ops):
                for fn in _ops:
                    fn(eng)
            getattr(blk, name)(body)


class DrainAll:

    def __init__(self, eng):
        self._e = eng

    def __getattr__(self, name):
        f = getattr(self._e, name)
        if name in ("wait_ge", "drain", "nop"):
            return f

        def wrapped(*a, **k):
            r = f(*a, **k)
            self._e.drain()
            return r
        return wrapped


def build_program(debug=False):
    nc = bass.Bass("TRN2", target_bir_lowering=False)
    if debug:
        dbg_smal_d = nc.dram_tensor("dbg_smal", [NH, 64], F32, kind="ExternalOutput").ap()
        dbg_pneg_d = nc.dram_tensor("dbg_pneg", [NH, 256], F32, kind="ExternalOutput").ap()
        dbg_ppos_d = nc.dram_tensor("dbg_ppos", [NH, 256], F32, kind="ExternalOutput").ap()
        dbg_pcar_d = nc.dram_tensor("dbg_pcar", [NH, 256], F32, kind="ExternalOutput").ap()
        dbg_bu_d = nc.dram_tensor("dbg_bu", [NH, 2 * T], F32, kind="ExternalOutput").ap()
        dbg_xs_d = nc.dram_tensor("dbg_xs", [NH, 2 * T], F32, kind="ExternalOutput").ap()
        dbg_x1_d = nc.dram_tensor("dbg_x1", [TP, D], F32, kind="ExternalOutput").ap()
        dbg_w_d = nc.dram_tensor("dbg_w", [128, TCP * 8], F32, kind="ExternalOutput").ap()
        dbg_ln2_d = nc.dram_tensor("dbg_ln2", [128, HC * TP], BF16, kind="ExternalOutput").ap()

    x_own_d = nc.dram_tensor("x_own", [T, D], F32, kind="ExternalInput").ap()
    x_prev_d = nc.dram_tensor("x_prev", [T, D], F32, kind="ExternalInput").ap()
    cmask_d = nc.dram_tensor("cmask", [NH, 1], F32, kind="ExternalInput").ap()
    lam_ur_d = nc.dram_tensor("lam_ur", [NH, 1], F32, kind="ExternalInput").ap()
    lam_im_d = nc.dram_tensor("lam_im", [NH, 1], F32, kind="ExternalInput").ap()
    logd_d = nc.dram_tensor("logd", [NH, 1], F32, kind="ExternalInput").ap()
    dvec_d = nc.dram_tensor("dvec", [1, D], F32, kind="ExternalInput").ap()
    bre_d = nc.dram_tensor("bre", [NH, D], F32, kind="ExternalInput").ap()
    bim_d = nc.dram_tensor("bim", [NH, D], F32, kind="ExternalInput").ap()
    cret_d = nc.dram_tensor("cret", [NH, D], F32, kind="ExternalInput").ap()
    cimt_d = nc.dram_tensor("cimt", [NH, D], F32, kind="ExternalInput").ap()
    rwt_d = nc.dram_tensor("rwt", [128, 128], F32, kind="ExternalInput").ap()
    wall_d = nc.dram_tensor("wall", [128, E * ECOLS], F32, kind="ExternalInput").ap()
    out_d = nc.dram_tensor("out", [T, D], F32, kind="ExternalOutput").ap()

    with ExitStack() as S:
        sem = {}
        for name in ("dbgst", "xld", "gld", "gsu", "sqd", "rstd", "ln1d", "trab", "evab", "bustop",
                     "bufree", "sv", "vv", "tt", "scand", "abdone",
                     "wld", "x2ld", "ymm", "yev", "ctr", "ltr", "cev",
                     "rmm", "rev", "topd", "wtr", "wtev",
                     "wbmm", "wbev", "gstop", "ustop", "silud", "thd",
                     "dnstop", "dnev", "outst"):
            sem[name] = Sem(S.enter_context(nc.semaphore(name)))

        def sb(name, shape, dt):
            return S.enter_context(nc.sbuf_tensor(name, shape, dt))

        xt = [sb(f"xt{i}", [128, D], F32) for i in range(2)]
        x1 = sb("x1", [128, TCP, D], F32)
        ln2t = sb("ln2t", [128, HC, TP], BF16)
        sg = [sb(f"sg{i}", [128, TP], F32) for i in range(2)]
        th = sb("th", [128, TP], F32)
        dbc = sb("dbc", [128, D], F32)
        xsbf = sb("xsbf", [NH, 2 * T], BF16)
        c2re = sb("c2re", [NH, D], BF16)
        c2im = sb("c2im", [NH, D], BF16)
        rwtb = sb("rwtb", [128, 128], BF16)
        bbtre = sb("bbtre", [128, HC * NH], BF16)
        bbtim = sb("bbtim", [128, HC * NH], BF16)
        ident = sb("ident", [128, 128], F32)
        ltbig = sb("ltbig", [128, 64], F32)
        oneh = sb("oneh", [8, E * 128], F32)
        onesr = sb("onesr", [1, 128], F32)
        onescan = sb("onescan", [NH, 128], F32)
        pneg = sb("pneg", [NH, 256], F32)
        ppos = sb("ppos", [NH, 256], F32)
        pcar = sb("pcar", [NH, 256], F32)
        smal = sb("smal", [NH, 64], F32)
        stat = sb("stat", [128, 8], F32)
        wsb = sb("wsb", [128, TCP * 8], F32)
        wtt = sb("wtt", [8, TP], F32)
        rfm = sb("rfm", [8, 128], F32)
        ltm = sb("ltm", [128, 8], F32)
        prob = sb("prob", [128, 8], F32)
        ptmp = sb("ptmp", [128, 64], F32)
        ptmp2 = sb("ptmp2", [128, 64], F32)
        pS = sb("pS", [128, 8], F32)

        SC_NAMES = ("lam_ur", "lam_im", "logd", "t1", "t2", "t3", "t4", "sp",
                    "lam_re", "delta", "ldre", "ldim", "mag", "m1", "g1",
                    "m2c", "g2", "m3", "s2", "s2sq", "cosv", "sinv", "lbre",
                    "lbim", "a1", "u1", "u2", "nre", "nim", "den", "rden",
                    "cofre", "cofim", "r2", "ir2", "invre", "invim",
                    "pwre", "pwim", "npre", "npim",
                    "l128re", "l128im", "cmask", "ccre", "ccim",
                    "tca", "tcb", "tcc", "carry_re", "carry_im")
        SC = {n: i for i, n in enumerate(SC_NAMES)}

        def sc(n):
            return smal[:, SC[n]:SC[n] + 1]

        pA = S.enter_context(nc.psum_tensor("pA", [128, 2048], F32))
        pB = S.enter_context(nc.psum_tensor("pB", [128, 2048], F32))

        P = Prog()
        xld, vv, sv, tt = sem["xld"], sem["vv"], sem["sv"], sem["tt"]

        def g_setup(g):
            g.memset(ident[:], 0.0)
            g.drain()
            g.affine_select(out=ident[:], in_=ident[:], compare_op=ALU.not_equal,
                            fill=1.0, base=0, pattern=[[-1, 128]], channel_multiplier=1)
            g.memset(ltbig[:], 0.0)
            for e in range(1, E):
                g.memset(ltbig[:, e * 8:e * 8 + e], 1.0)
            g.memset(oneh[:], 0.0).then_inc(sem["gsu"].h, 1)
        P.em("gpsimd", g_setup)
        GSU = sem["gsu"].inc()

        def s_smalls(sy):
            for dst, src in ((sc("lam_ur"), lam_ur_d), (sc("lam_im"), lam_im_d),
                             (sc("logd"), logd_d), (sc("cmask"), cmask_d),
                             (dbc[0:1, :], dvec_d)):
                sy.dma_start(dst, src).then_inc(xld.h, 16)
        P.em("sync", s_smalls)
        SMALLS = xld.inc(16 * 5)

        P.wait("sync", sem["gsu"], GSU)
        P.wait("sync", vv, 1)

        def s_oneh(sy):
            for e in range(E):
                sy.dma_start(oneh[e:e + 1, e * 128:(e + 1) * 128],
                             onesr[0:1, 0:128]).then_inc(xld.h, 16)
        P.em("sync", s_oneh)
        OH_LD = xld.inc(16 * 8)

        def g_cast_loads(g):
            g.dma_start(c2re[:], cret_d).then_inc(sem["gld"].h, 16)
            g.dma_start(c2im[:], cimt_d).then_inc(sem["gld"].h, 16)
            g.dma_start(rwtb[:], rwt_d).then_inc(sem["gld"].h, 16)
        P.em("gpsimd", g_cast_loads)
        GLD = sem["gld"].inc(16 * 3)

        def v_consts(v):
            v.memset(onesr[:], 1.0)
            v.memset(onescan[:], 1.0).then_inc(vv.h, 1)
        P.em("vector", v_consts)
        V_CONST = vv.inc()

        P.wait("scalar", xld, SMALLS)

        def s_set1(s):
            s.activation(sc("t1"), sc("lam_ur"), AF.Abs)
            s.drain()
            s.activation(sc("t2"), sc("t1"), AF.Exp, scale=-1.0)
            s.drain()
            s.activation(sc("t3"), sc("t2"), AF.Ln, bias=1.0)
            s.activation(sc("t4"), sc("lam_ur"), AF.Relu)
            s.activation(sc("delta"), sc("logd"), AF.Exp).then_inc(sv.h, 1)
        P.em("scalar", s_set1)
        S_SET1 = sv.inc()

        P.wait("vector", sv, S_SET1)

        def v_set1(v):
            v.tensor_tensor(sc("sp"), sc("t4"), sc("t3"), ALU.add)
            v.drain()
            v.tensor_scalar(sc("lam_re"), sc("sp"), 0.0101, -1.0, ALU.add, ALU.mult)
            v.drain()
            v.tensor_tensor(sc("ldre"), sc("lam_re"), sc("delta"), ALU.mult)
            v.tensor_tensor(sc("ldim"), sc("lam_im"), sc("delta"), ALU.mult)
            MAGIC = 12582912.0
            v.drain()
            v.tensor_single_scalar(sc("m1"), sc("ldim"), 1.0 / TWO_PI, ALU.mult)
            v.drain()
            v.tensor_single_scalar(sc("g1"), sc("m1"), MAGIC, ALU.add)
            v.drain()
            v.tensor_single_scalar(sc("g2"), sc("g1"), MAGIC, ALU.subtract)
            v.drain()
            v.tensor_single_scalar(sc("m2c"), sc("g2"), -TWO_PI, ALU.mult)
            v.drain()
            v.tensor_tensor(sc("m3"), sc("ldim"), sc("m2c"), ALU.add).then_inc(vv.h, 1)
        P.em("vector", v_set1)
        V_SET1 = vv.inc()

        P.wait("scalar", vv, V_SET1)

        def s_set2(s):
            s.activation(sc("mag"), sc("ldre"), AF.Exp)
            s.activation(sc("s2"), sc("m3"), AF.Sin, scale=0.5)
            s.activation(sc("sinv"), sc("m3"), AF.Sin).then_inc(sv.h, 1)
        P.em("scalar", s_set2)
        S_SET2 = sv.inc()

        P.wait("vector", sv, S_SET2)
        P.wait("vector", xld, SMALLS)

        def v_set2(v):
            v = DrainAll(v)
            v.tensor_tensor(sc("s2sq"), sc("s2"), sc("s2"), ALU.mult)
            v.tensor_scalar(sc("cosv"), sc("s2sq"), -2.0, 1.0, ALU.mult, ALU.add)
            v.tensor_tensor(sc("lbre"), sc("mag"), sc("cosv"), ALU.mult)
            v.tensor_tensor(sc("lbim"), sc("mag"), sc("sinv"), ALU.mult)
            v.tensor_single_scalar(sc("a1"), sc("lbre"), 1.0, ALU.subtract)
            v.tensor_tensor(sc("u1"), sc("a1"), sc("lam_re"), ALU.mult)
            v.tensor_tensor(sc("u2"), sc("lbim"), sc("lam_im"), ALU.mult)
            v.tensor_tensor(sc("nre"), sc("u1"), sc("u2"), ALU.add)
            v.tensor_tensor(sc("u1"), sc("lbim"), sc("lam_re"), ALU.mult)
            v.tensor_tensor(sc("u2"), sc("a1"), sc("lam_im"), ALU.mult)
            v.tensor_tensor(sc("nim"), sc("u1"), sc("u2"), ALU.subtract)
            v.tensor_tensor(sc("u1"), sc("lam_re"), sc("lam_re"), ALU.mult)
            v.tensor_tensor(sc("u2"), sc("lam_im"), sc("lam_im"), ALU.mult)
            v.tensor_tensor(sc("den"), sc("u1"), sc("u2"), ALU.add)
            v.reciprocal(sc("rden"), sc("den"))
            v.tensor_tensor(sc("cofre"), sc("nre"), sc("rden"), ALU.mult)
            v.tensor_tensor(sc("cofim"), sc("nim"), sc("rden"), ALU.mult)
            v.tensor_tensor(sc("u1"), sc("lbre"), sc("lbre"), ALU.mult)
            v.tensor_tensor(sc("u2"), sc("lbim"), sc("lbim"), ALU.mult)
            v.tensor_tensor(sc("r2"), sc("u1"), sc("u2"), ALU.add)
            v.reciprocal(sc("ir2"), sc("r2"))
            v.tensor_tensor(sc("invre"), sc("lbre"), sc("ir2"), ALU.mult)
            v.tensor_tensor(sc("invim"), sc("lbim"), sc("ir2"), ALU.mult)
            v.tensor_single_scalar(sc("invim"), sc("invim"), -1.0, ALU.mult).then_inc(vv.h, 1)
        P.em("vector", v_set2)
        V_SET2 = vv.inc()

        P.wait("tensor", vv, V_CONST)
        P.wait("tensor", xld, SMALLS)

        def t_dbc(t):
            for j in range(4):
                i = t.matmul(pA[:, j * 512:(j + 1) * 512], onesr[:],
                             dbc[0:1, j * 512:(j + 1) * 512], start=True, stop=True)
            i.then_inc(tt.h, 1)
        P.em("tensor", t_dbc)
        T_DBC = tt.inc()

        P.wait("vector", tt, T_DBC)

        def v_dbc(v):
            v.tensor_copy(dbc[:], pA[:]).then_inc(vv.h, 1)
        P.em("vector", v_dbc)
        V_DBC = vv.inc()

        with ExitStack() as S_ab:
            def sba(name, shape, dt):
                return S_ab.enter_context(nc.sbuf_tensor(name, shape, dt))
            bre_sb = sba("bre_sb", [NH, D], F32)
            bim_sb = sba("bim_sb", [NH, D], F32)
            ln1b = [sba(f"ln1b{i}", [128, D], F32) for i in range(2)]
            lntb = [sba(f"lntb{i}", [128, D], BF16) for i in range(2)]
            busb = sba("busb", [NH, 2 * T], F32)
            usb = sba("usb", [NH, 2 * T], F32)
            cssb = sba("cssb", [NH, 2 * T], F32)
            qts = sba("qts", [NH, 256], F32)
            carl = sba("carl", [NH, 96], F32)
            xsqb = x1[:, 0, :]
            bbre = cssb[:, 0:D]
            bbim = cssb[:, D:2 * D]

            P.wait("vector", sv, S_SET2)

            def v_tables(v):
                v = DrainAll(v)
                def build_table(tab, base_re, base_im):
                    v.memset(tab[:, 0:1], 1.0)
                    v.memset(tab[:, 128:129], 0.0)
                    v.tensor_copy(sc("pwre"), base_re)
                    v.tensor_copy(sc("pwim"), base_im)
                    ln_ = 1
                    while ln_ < 128:
                        re0, im0 = tab[:, 0:ln_], tab[:, 128:128 + ln_]
                        re1, im1 = tab[:, ln_:2 * ln_], tab[:, 128 + ln_:128 + 2 * ln_]
                        ta, tb = qts[:, 0:ln_], qts[:, 128:128 + ln_]
                        v.tensor_scalar(ta, re0, sc("pwre"), None, ALU.mult)
                        v.tensor_scalar(tb, im0, sc("pwim"), None, ALU.mult)
                        v.tensor_tensor(re1, ta, tb, ALU.subtract)
                        v.tensor_scalar(ta, re0, sc("pwim"), None, ALU.mult)
                        v.tensor_scalar(tb, im0, sc("pwre"), None, ALU.mult)
                        v.tensor_tensor(im1, ta, tb, ALU.add)
                        if 2 * ln_ < 128:
                            v.tensor_tensor(sc("npre"), sc("pwre"), sc("pwre"), ALU.mult)
                            v.tensor_tensor(sc("npim"), sc("pwim"), sc("pwim"), ALU.mult)
                            v.tensor_tensor(sc("npre"), sc("npre"), sc("npim"), ALU.subtract)
                            v.tensor_tensor(sc("npim"), sc("pwre"), sc("pwim"), ALU.mult)
                            v.tensor_single_scalar(sc("npim"), sc("npim"), 2.0, ALU.mult)
                            v.tensor_copy(sc("pwre"), sc("npre"))
                            v.tensor_copy(sc("pwim"), sc("npim"))
                        ln_ *= 2
                build_table(pneg, sc("invre"), sc("invim"))
                build_table(ppos, sc("lbre"), sc("lbim"))
                ta, tb = qts[:, 0:128], qts[:, 128:256]
                v.tensor_scalar(ta, ppos[:, 0:128], sc("lbre"), None, ALU.mult)
                v.tensor_scalar(tb, ppos[:, 128:256], sc("lbim"), None, ALU.mult)
                v.tensor_tensor(pcar[:, 0:128], ta, tb, ALU.subtract)
                v.tensor_scalar(ta, ppos[:, 0:128], sc("lbim"), None, ALU.mult)
                v.tensor_scalar(tb, ppos[:, 128:256], sc("lbre"), None, ALU.mult)
                v.tensor_tensor(pcar[:, 128:256], ta, tb, ALU.add)
                v.tensor_tensor(sc("tca"), ppos[:, 127:128], sc("lbre"), ALU.mult)
                v.tensor_tensor(sc("tcb"), ppos[:, 255:256], sc("lbim"), ALU.mult)
                v.tensor_tensor(sc("l128re"), sc("tca"), sc("tcb"), ALU.subtract)
                v.tensor_tensor(sc("tca"), ppos[:, 127:128], sc("lbim"), ALU.mult)
                v.tensor_tensor(sc("tcb"), ppos[:, 255:256], sc("lbre"), ALU.mult)
                v.tensor_tensor(sc("l128im"), sc("tca"), sc("tcb"), ALU.add).then_inc(vv.h, 1)
            P.em("vector", v_tables)
            V_TABLES = vv.inc()

            def s_bload(sy):
                sy.dma_start(bre_sb[:], bre_d).then_inc(xld.h, 16)
                sy.dma_start(bim_sb[:], bim_d).then_inc(xld.h, 16)
            P.em("sync", s_bload)
            BLD = xld.inc(32)

            P.wait("vector", xld, BLD)

            def v_bbar(v):
                v = DrainAll(v)
                w1, w2 = busb[:, 0:D], busb[:, D:2 * D]
                v.tensor_scalar(w1, bre_sb[:], sc("cofre"), None, ALU.mult)
                v.tensor_scalar(w2, bim_sb[:], sc("cofim"), None, ALU.mult)
                v.tensor_tensor(bbre, w1, w2, ALU.subtract)
                v.tensor_scalar(w1, bre_sb[:], sc("cofim"), None, ALU.mult)
                v.tensor_scalar(w2, bim_sb[:], sc("cofre"), None, ALU.mult)
                v.tensor_tensor(bbim, w1, w2, ALU.add).then_inc(vv.h, 1)
            P.em("vector", v_bbar)
            V_BBAR = vv.inc()

            P.wait("tensor", vv, V_BBAR)

            def t_bbt(t):
                for c in range(HC):
                    t.transpose(pA[:, c * 32:(c + 1) * 32],
                                cssb[:, c * 128:(c + 1) * 128], ident[0:32, 0:32])
                for c in range(HC):
                    i = t.transpose(pA[:, 512 + c * 32:512 + (c + 1) * 32],
                                    cssb[:, D + c * 128:D + (c + 1) * 128], ident[0:32, 0:32])
                i.then_inc(tt.h, 1)
            P.em("tensor", t_bbt)
            T_BBT = tt.inc()

            P.wait("vector", tt, T_BBT)

            def v_bbt(v):
                v.tensor_copy(bbtre[:], pA[:, 0:512])
                v.tensor_copy(bbtim[:], pA[:, 512:1024]).then_inc(vv.h, 1)
            P.em("vector", v_bbt)
            V_BBT = vv.inc()

            stm, stm2, stvar, strst = (stat[:, i:i + 1] for i in range(4))

            for ti in range(32):
                src = x_prev_d if ti < 16 else x_own_d
                row = (ti % 16) * 128
                buf = ti % 2

                P.wait("sync", sem["ln1d"], ti - 1)
                tgt = xld.inc(16)

                def s_ld(sy, _s=src, _r=row, _b=buf):
                    sy.dma_start(xt[_b][:], _s[_r:_r + 128, :]).then_inc(xld.h, 16)
                P.em("sync", s_ld)

                P.wait("scalar", xld, tgt)
                P.wait("scalar", sem["scand"], ti)

                def s_sq(s, _b=buf):
                    s.activation(xsqb, xt[_b][:], AF.Square).then_inc(sem["sqd"].h, 1)
                P.em("scalar", s_sq)
                sqt = sem["sqd"].inc()

                P.wait("vector", xld, tgt)
                P.wait("vector", sem["sqd"], sqt)

                def v_stats(v, _b=buf):
                    v = DrainAll(v)
                    v.reduce_sum(stm[:], xt[_b][:], axis=AX.X)
                    v.tensor_single_scalar(stm[:], stm[:], INV_D, ALU.mult)
                    v.reduce_sum(stvar[:], xsqb, axis=AX.X)
                    v.tensor_tensor(stm2[:], stm[:], stm[:], ALU.mult)
                    v.tensor_single_scalar(stvar[:], stvar[:], INV_D, ALU.mult)
                    v.tensor_tensor(stvar[:], stvar[:], stm2[:], ALU.subtract)
                    v.tensor_single_scalar(stvar[:], stvar[:], EPS, ALU.add).then_inc(sem["scand"].h, 1)
                P.em("vector", v_stats)
                vstat = sem["scand"].inc()

                P.wait("scalar", sem["scand"], vstat)

                def s_rstd(s):
                    s.activation(strst[:], stvar[:], AF.Ln)
                    s.drain()
                    s.activation(strst[:], strst[:], AF.Exp, scale=-0.5).then_inc(sem["rstd"].h, 1)
                P.em("scalar", s_rstd)
                rst = sem["rstd"].inc()

                P.wait("vector", sem["rstd"], rst)

                def v_ln1(v, _b=buf):
                    v.tensor_scalar(ln1b[_b][:], xt[_b][:], stm[:], strst[:],
                                    ALU.subtract, ALU.mult).then_inc(sem["ln1d"].h, 1)
                P.em("vector", v_ln1)
                ln1t = sem["ln1d"].inc()

                P.wait("tensor", sem["ln1d"], ln1t)
                if ti == 0:
                    P.wait("tensor", vv, V_BBT)
                else:
                    P.wait("tensor", sem["evab"], 2 * ti)

                def t_tr(t, _b=buf):
                    for c in range(8):
                        i1 = t.transpose(pA[:, c * 128:(c + 1) * 128],
                                         ln1b[_b][:, c * 128:(c + 1) * 128], ident[:])
                    i1.then_inc(sem["trab"].h, 1)
                    for c in range(8, 16):
                        i2 = t.transpose(pA[:, c * 128:(c + 1) * 128],
                                         ln1b[_b][:, c * 128:(c + 1) * 128], ident[:])
                    i2.then_inc(sem["trab"].h, 1)
                P.em("tensor", t_tr)
                trt = sem["trab"].inc(2)

                P.wait("vector", sem["bustop"], 2 * (ti - 1))

                def v_ev(v, _b=buf, _t=trt):
                    v.wait_ge(sem["trab"].h, _t - 1)
                    v.tensor_copy(lntb[_b][:, 0:1024], pA[:, 0:1024]).then_inc(sem["evab"].h, 1)
                    v.wait_ge(sem["trab"].h, _t)
                    v.tensor_copy(lntb[_b][:, 1024:2048], pA[:, 1024:2048]).then_inc(sem["evab"].h, 1)
                P.em("vector", v_ev)
                evt = sem["evab"].inc(2)

                P.wait("tensor", sem["evab"], evt)
                P.wait("tensor", sem["bufree"], ti)

                def t_bu(t, _b=buf):
                    for c in range(HC):
                        t.matmul(pB[0:32, 0:128], bbtre[:, c * 32:(c + 1) * 32],
                                 lntb[_b][:, c * 128:(c + 1) * 128],
                                 start=(c == 0), stop=(c == HC - 1))
                    for c in range(HC):
                        i = t.matmul(pB[0:32, 512:640], bbtim[:, c * 32:(c + 1) * 32],
                                     lntb[_b][:, c * 128:(c + 1) * 128],
                                     start=(c == 0), stop=(c == HC - 1))
                    i.then_inc(sem["bustop"].h, 2)
                P.em("tensor", t_bu)
                but = sem["bustop"].inc(2)

                tcol = (ti % 16) * 128
                P.wait("vector", sem["bustop"], but)

                def v_bu(v, _tc=tcol):
                    v.tensor_copy(busb[:, _tc:_tc + 128], pB[0:32, 0:128])
                    v.tensor_copy(busb[:, T + _tc:T + _tc + 128],
                                  pB[0:32, 512:640]).then_inc(sem["bufree"].h, 1)
                P.em("vector", v_bu)
                sem["bufree"].inc()

                if ti in (15, 31):
                    own = ti == 31

                    def v_scan(v, _own=own):
                        v = DrainAll(v)
                        bre, bim = busb[:, 0:T], busb[:, T:2 * T]
                        ure, uim = usb[:, 0:T], usb[:, T:2 * T]
                        csre, csim = cssb[:, 0:T], cssb[:, T:2 * T]
                        pnre, pnim = pneg[:, 0:128], pneg[:, 128:256]
                        ta, tb = qts[:, 0:128], qts[:, 128:256]
                        for c in range(CH):
                            s0, s1 = c * 128, (c + 1) * 128
                            v.tensor_tensor(ta, bre[:, s0:s1], pnre, ALU.mult)
                            v.tensor_tensor(tb, bim[:, s0:s1], pnim, ALU.mult)
                            v.tensor_tensor(ure[:, s0:s1], ta, tb, ALU.subtract)
                            v.tensor_tensor(ta, bre[:, s0:s1], pnim, ALU.mult)
                            v.tensor_tensor(tb, bim[:, s0:s1], pnre, ALU.mult)
                            v.tensor_tensor(uim[:, s0:s1], ta, tb, ALU.add)
                        for c in range(CH):
                            s0, s1 = c * 128, (c + 1) * 128
                            v.tensor_tensor_scan(csre[:, s0:s1], onescan[:], ure[:, s0:s1],
                                                 0.0, ALU.mult, ALU.add)
                            v.tensor_tensor_scan(csim[:, s0:s1], onescan[:], uim[:, s0:s1],
                                                 0.0, ALU.mult, ALU.add)
                        cre_s = cssb[:, 127:T:128]
                        cim_s = cssb[:, T + 127:2 * T:128]
                        car_re, car_im = carl[:, 0:16], carl[:, 16:32]
                        t16a, t16b = carl[:, 32:48], carl[:, 48:64]
                        v.tensor_scalar(t16a, cre_s, ppos[:, 127:128], None, ALU.mult)
                        v.tensor_scalar(t16b, cim_s, ppos[:, 255:256], None, ALU.mult)
                        v.tensor_tensor(car_re, t16a, t16b, ALU.subtract)
                        v.tensor_scalar(t16a, cre_s, ppos[:, 255:256], None, ALU.mult)
                        v.tensor_scalar(t16b, cim_s, ppos[:, 127:128], None, ALU.mult)
                        v.tensor_tensor(car_im, t16a, t16b, ALU.add)
                        if not _own:
                            v.memset(sc("ccre"), 0.0)
                            v.memset(sc("ccim"), 0.0)
                        else:
                            v.tensor_tensor(sc("ccre"), sc("carry_re"), sc("cmask"), ALU.mult)
                            v.tensor_tensor(sc("ccim"), sc("carry_im"), sc("cmask"), ALU.mult)
                        ccm_re, ccm_im = carl[:, 64:80], carl[:, 80:96]
                        for c in range(CH):
                            v.tensor_copy(ccm_re[:, c:c + 1], sc("ccre"))
                            v.tensor_copy(ccm_im[:, c:c + 1], sc("ccim"))
                            v.tensor_tensor(sc("tca"), sc("ccre"), sc("l128re"), ALU.mult)
                            v.tensor_tensor(sc("tcb"), sc("ccim"), sc("l128im"), ALU.mult)
                            v.tensor_tensor(sc("tcc"), sc("tca"), sc("tcb"), ALU.subtract)
                            v.tensor_tensor(sc("tca"), sc("ccre"), sc("l128im"), ALU.mult)
                            v.tensor_tensor(sc("tcb"), sc("ccim"), sc("l128re"), ALU.mult)
                            v.tensor_tensor(sc("ccim"), sc("tca"), sc("tcb"), ALU.add)
                            v.tensor_tensor(sc("ccim"), sc("ccim"), car_im[:, c:c + 1], ALU.add)
                            v.tensor_tensor(sc("ccre"), sc("tcc"), car_re[:, c:c + 1], ALU.add)
                        if not _own:
                            v.tensor_copy(sc("carry_re"), sc("ccre"))
                            v.tensor_copy(sc("carry_im"), sc("ccim"))
                            return
                        ppre, ppim = ppos[:, 0:128], ppos[:, 128:256]
                        pcre, pcim = pcar[:, 0:128], pcar[:, 128:256]
                        for c in range(CH):
                            s0, s1 = c * 128, (c + 1) * 128
                            v.tensor_tensor(ta, csre[:, s0:s1], ppre, ALU.mult)
                            v.tensor_tensor(tb, csim[:, s0:s1], ppim, ALU.mult)
                            v.tensor_tensor(ure[:, s0:s1], ta, tb, ALU.subtract)
                            v.tensor_tensor(ta, csre[:, s0:s1], ppim, ALU.mult)
                            v.tensor_tensor(tb, csim[:, s0:s1], ppre, ALU.mult)
                            v.tensor_tensor(uim[:, s0:s1], ta, tb, ALU.add)
                            v.tensor_scalar(ta, pcre, ccm_re[:, c:c + 1], None, ALU.mult)
                            v.tensor_scalar(tb, pcim, ccm_im[:, c:c + 1], None, ALU.mult)
                            v.tensor_tensor(ta, ta, tb, ALU.subtract)
                            v.tensor_tensor(ure[:, s0:s1], ure[:, s0:s1], ta, ALU.add)
                            v.tensor_scalar(ta, pcre, ccm_im[:, c:c + 1], None, ALU.mult)
                            v.tensor_scalar(tb, pcim, ccm_re[:, c:c + 1], None, ALU.mult)
                            v.tensor_tensor(ta, ta, tb, ALU.add)
                            v.tensor_tensor(uim[:, s0:s1], uim[:, s0:s1], ta, ALU.add)
                        v.tensor_single_scalar(xsbf[:, 0:T], usb[:, 0:T], 2.0, ALU.mult)
                        v.tensor_single_scalar(xsbf[:, T:2 * T], usb[:, T:2 * T],
                                               -2.0, ALU.mult).then_inc(sem["abdone"].h, 1)
                    P.em("vector", v_scan)
                    if own:
                        sem["abdone"].inc()

            if debug:
                P.wait("sync", sem["abdone"], 1)

                def s_dbg1(sy):
                    sy.dma_start(dbg_smal_d, smal[:]).then_inc(sem["dbgst"].h, 16)
                    sy.dma_start(dbg_pneg_d, pneg[:]).then_inc(sem["dbgst"].h, 16)
                    sy.dma_start(dbg_ppos_d, ppos[:]).then_inc(sem["dbgst"].h, 16)
                    sy.dma_start(dbg_pcar_d, pcar[:]).then_inc(sem["dbgst"].h, 16)
                    sy.dma_start(dbg_bu_d, busb[:]).then_inc(sem["dbgst"].h, 16)
                    sy.dma_start(dbg_xs_d, usb[:]).then_inc(sem["dbgst"].h, 16)
                    sy.wait_ge(sem["dbgst"].h, 16 * 6)
                P.em("sync", s_dbg1)
                sem["dbgst"].inc(16 * 6)
            with nc.Block() as blk1:
                P.replay(blk1)

        pool = [sb(f"wu{i}", [128, UNIT], BF16) for i in range(POOL)]
        wball = sb("wball", [128, E, TP], BF16)
        hid = sb("hid", [128, IC, TP], BF16)
        lnc = sb("lnc", [128, D], F32)
        xsqc = sb("xsqc", [128, D], F32)

        P2 = Prog()
        W = P2.wait
        EM = P2.em

        def gu_iter(p, e, ic):
            return ((p * E) + e) * IC + ic

        def dnev_after(p, e, hq):
            return (((p * E) + e) * HQ + hq + 1) * TCP

        unit_free_req = []
        for p in range(NPASS):
            for e in range(E):
                for k in range(16):
                    unit_free_req.append(("thd", gu_iter(p, e, 2 * k + 1) + 1))
                    unit_free_req.append(("thd", gu_iter(p, e, 2 * k + 1) + 1))
                for k in range(16):
                    unit_free_req.append(("dnev", dnev_after(p, e, k // 4)))

        def unit_cols(e, k):
            base = e * ECOLS
            if k < GU_UNITS:
                pair = k // 2
                return base + (0 if k % 2 == 0 else 65536) + pair * UNIT
            return base + 2 * 65536 + (k - GU_UNITS) * UNIT

        def g_stream(g):
            g.wait_ge(sem["abdone"].h, 1)
            if debug:
                g.wait_ge(sem["dbgst"].h, 16 * 6)
            n = 0
            for p in range(NPASS):
                for e in range(E):
                    for k in range(NUNIT_E):
                        if n >= POOL:
                            sname, val = unit_free_req[n - POOL]
                            g.wait_ge(sem[sname].h, val)
                        cols = unit_cols(e, k)
                        g.dma_start(pool[n % POOL][:],
                                    wall_d[:, cols:cols + UNIT]).then_inc(sem["wld"].h, 16)
                        n += 1
        EM("gpsimd", g_stream)
        sem["wld"].inc(16 * NPASS * E * NUNIT_E)

        stm, stm2, stvar, strst = (stat[:, i:i + 1] for i in range(4))
        nmx = stat[:, 4:5]
        ssum = stat[:, 5:6]
        rsum = stat[:, 6:7]

        for p in range(NPASS):
            for tc in range(TCP):
                ti = p * TCP + tc
                buf = ti % 2
                row = ti * 128

                if ti < 2:
                    W("sync", sem["ln1d"], 31 + ti)
                else:
                    W("sync", sem["yev"], ti - 1)
                xt_tgt = sem["x2ld"].inc(16)

                def s_xld(sy, _r=row, _b=buf):
                    sy.dma_start(xt[_b][:], x_own_d[_r:_r + 128, :]).then_inc(sem["x2ld"].h, 16)
                EM("sync", s_xld)

                W("tensor", sem["abdone"], 1)
                if ti > 0:
                    W("tensor", sem["cev"], ti)
                if p > 0 and tc == 0:
                    W("tensor", sem["thd"], p * E * IC)

                def t_y(t, _ti=ti):
                    col = _ti * 128
                    for j in range(4):
                        t.matmul(pA[:, j * 512:(j + 1) * 512], xsbf[:, col:col + 128],
                                 c2re[:, j * 512:(j + 1) * 512], start=True, stop=False,
                                 skip_group_check=True)
                    for j in range(4):
                        i = t.matmul(pA[:, j * 512:(j + 1) * 512],
                                     xsbf[:, T + col:T + col + 128],
                                     c2im[:, j * 512:(j + 1) * 512], start=False, stop=True,
                                     skip_group_check=True)
                    i.then_inc(sem["ymm"].h, 1)
                EM("tensor", t_y)
                ymt = sem["ymm"].inc()

                W("scalar", sem["x2ld"], xt_tgt)
                W("scalar", sem["scand"], sem["scand"].n)

                def s_sq1(s, _b=buf):
                    s.activation(xsqc[:], xt[_b][:], AF.Square).then_inc(sem["sqd"].h, 1)
                EM("scalar", s_sq1)
                sq1 = sem["sqd"].inc()

                W("vector", sem["x2ld"], xt_tgt)
                W("vector", sem["sqd"], sq1)

                def v_st1(v, _b=buf):
                    v = DrainAll(v)
                    v.reduce_sum(stm[:], xt[_b][:], axis=AX.X)
                    v.tensor_single_scalar(stm[:], stm[:], INV_D, ALU.mult)
                    v.reduce_sum(stvar[:], xsqc[:], axis=AX.X)
                    v.tensor_tensor(stm2[:], stm[:], stm[:], ALU.mult)
                    v.tensor_single_scalar(stvar[:], stvar[:], INV_D, ALU.mult)
                    v.tensor_tensor(stvar[:], stvar[:], stm2[:], ALU.subtract)
                    v.tensor_single_scalar(stvar[:], stvar[:], EPS, ALU.add).then_inc(sem["scand"].h, 1)
                EM("vector", v_st1)
                st1 = sem["scand"].inc()

                W("scalar", sem["scand"], st1)

                def s_rstd1(s):
                    s.activation(strst[:], stvar[:], AF.Ln)
                    s.drain()
                    s.activation(strst[:], strst[:], AF.Exp, scale=-0.5).then_inc(sem["rstd"].h, 1)
                EM("scalar", s_rstd1)
                rs1 = sem["rstd"].inc()

                W("vector", sem["rstd"], rs1)
                W("vector", sem["ymm"], ymt)
                W("vector", sem["ctr"], ti)
                if p > 0:
                    W("vector", sem["outst"], 16 * ((p - 1) * TCP + tc + 1))

                def v_x1(v, _b=buf, _tc=tc):
                    v = DrainAll(v)
                    v.tensor_scalar(lnc[:], xt[_b][:], stm[:], strst[:], ALU.subtract, ALU.mult)
                    v.tensor_tensor(lnc[:], lnc[:], dbc[:], ALU.mult)
                    v.tensor_tensor(x1[:, _tc, :], xt[_b][:], pA[:], ALU.add)
                    v.tensor_tensor(x1[:, _tc, :], x1[:, _tc, :], lnc[:], ALU.add).then_inc(sem["yev"].h, 1)
                EM("vector", v_x1)
                yevt = sem["yev"].inc()

                W("scalar", sem["yev"], yevt)

                def s_sq2(s, _tc=tc):
                    s.activation(xsqc[:], x1[:, _tc, :], AF.Square).then_inc(sem["sqd"].h, 1)
                EM("scalar", s_sq2)
                sq2 = sem["sqd"].inc()

                W("vector", sem["sqd"], sq2)

                def v_st2(v, _tc=tc):
                    v = DrainAll(v)
                    v.reduce_sum(stm[:], x1[:, _tc, :], axis=AX.X)
                    v.tensor_single_scalar(stm[:], stm[:], INV_D, ALU.mult)
                    v.reduce_sum(stvar[:], xsqc[:], axis=AX.X)
                    v.tensor_tensor(stm2[:], stm[:], stm[:], ALU.mult)
                    v.tensor_single_scalar(stvar[:], stvar[:], INV_D, ALU.mult)
                    v.tensor_tensor(stvar[:], stvar[:], stm2[:], ALU.subtract)
                    v.tensor_single_scalar(stvar[:], stvar[:], EPS, ALU.add).then_inc(sem["scand"].h, 1)
                EM("vector", v_st2)
                st2 = sem["scand"].inc()

                W("scalar", sem["scand"], st2)

                def s_rstd2(s):
                    s.activation(strst[:], stvar[:], AF.Ln)
                    s.drain()
                    s.activation(strst[:], strst[:], AF.Exp, scale=-0.5).then_inc(sem["rstd"].h, 1)
                EM("scalar", s_rstd2)
                rs2 = sem["rstd"].inc()

                W("vector", sem["rstd"], rs2)

                def v_ln2(v, _tc=tc):
                    v.tensor_scalar(lnc[:], x1[:, _tc, :], stm[:], strst[:],
                                    ALU.subtract, ALU.mult).then_inc(sem["scand"].h, 1)
                EM("vector", v_ln2)
                ln2r = sem["scand"].inc()

                W("tensor", sem["scand"], ln2r)
                W("tensor", sem["yev"], yevt)

                def t_ctr(t):
                    for c in range(HC):
                        i = t.transpose(pA[:, c * 128:(c + 1) * 128],
                                        lnc[:, c * 128:(c + 1) * 128], ident[:])
                    i.then_inc(sem["ctr"].h, 1)
                EM("tensor", t_ctr)
                ctrt = sem["ctr"].inc()

                W("vector", sem["ctr"], ctrt)

                def v_cev(v, _tc=tc):
                    for g4 in range(4):
                        src = pA[:, g4 * 512:(g4 + 1) * 512].rearrange("p (a b) -> p a b", a=4)
                        dst = ln2t[:, 4 * g4:4 * g4 + 4, _tc * 128:(_tc + 1) * 128]
                        i = v.tensor_copy(dst, src)
                    i.then_inc(sem["cev"].h, 1)
                EM("vector", v_cev)
                cevt = sem["cev"].inc()

                W("tensor", sem["cev"], cevt)
                if ti > 0:
                    W("tensor", sem["wtev"], ti)
                if p > 0 and tc == 0:
                    W("tensor", sem["dnev"], dnev_after(p - 1, E - 1, HQ - 1))
                    W("tensor", sem["wbev"], p * E)

                def t_r(t, _tc=tc):
                    for c in range(HC):
                        i = t.matmul(pB[0:8, 0:128], rwtb[:, c * 8:(c + 1) * 8],
                                     ln2t[:, c, _tc * 128:(_tc + 1) * 128],
                                     start=(c == 0), stop=(c == HC - 1))
                    i.then_inc(sem["rmm"].h, 1)
                EM("tensor", t_r)
                rmt = sem["rmm"].inc()

                W("vector", sem["rmm"], rmt)

                def v_rev(v):
                    v.tensor_copy(rfm[:], pB[0:8, 0:128]).then_inc(sem["rev"].h, 1)
                EM("vector", v_rev)
                revt = sem["rev"].inc()

                W("tensor", sem["rev"], revt)

                def t_lt(t):
                    t.transpose(pB[:, 512:520], rfm[:], ident[0:8, 0:8]).then_inc(sem["ltr"].h, 1)
                EM("tensor", t_lt)
                ltt = sem["ltr"].inc()

                W("vector", sem["ltr"], ltt)

                def v_sm1(v):
                    v = DrainAll(v)
                    v.tensor_copy(ltm[:], pB[:, 512:520])
                    v.reduce_max(nmx[:], ltm[:], axis=AX.X)
                    v.tensor_single_scalar(nmx[:], nmx[:], -1.0, ALU.mult).then_inc(sem["scand"].h, 1)
                EM("vector", v_sm1)
                sm1 = sem["scand"].inc()

                W("scalar", sem["scand"], sm1)

                def s_exp(s):
                    s.activation(prob[:], ltm[:], AF.Exp, bias=nmx[:]).then_inc(sv.h, 1)
                EM("scalar", s_exp)
                sxp = sv.inc()

                W("vector", sv, sxp)

                def v_top(v, _tc=tc):
                    v = DrainAll(v)
                    v.reduce_sum(ssum[:], prob[:], axis=AX.X)
                    v.reciprocal(rsum[:], ssum[:])
                    v.tensor_scalar(prob[:], prob[:], rsum[:], None, ALU.mult)
                    for e in range(E):
                        pc = prob[:, e:e + 1]
                        v.tensor_scalar(ptmp[:, e * 8:(e + 1) * 8], prob[:], pc, None, ALU.is_gt)
                        v.tensor_scalar(ptmp2[:, e * 8:(e + 1) * 8], prob[:], pc, None, ALU.is_equal)
                    v.tensor_tensor(ptmp2[:], ptmp2[:], ltbig[:], ALU.mult)
                    v.tensor_tensor(ptmp[:], ptmp[:], ptmp2[:], ALU.add)
                    for e in range(E):
                        v.tensor_tensor(ptmp[:, e * 8:(e + 1) * 8], ptmp[:, e * 8:(e + 1) * 8],
                                        prob[:], ALU.mult)
                    pt3 = ptmp[:].rearrange("p (a b) -> p a b", a=8)
                    v.tensor_reduce(pS[:], pt3, axis=AX.X, op=ALU.add)
                    v.tensor_single_scalar(pS[:], pS[:], 0.9, ALU.is_le)
                    v.tensor_tensor(wsb[:, _tc * 8:(_tc + 1) * 8], pS[:], prob[:],
                                    ALU.mult).then_inc(sem["topd"].h, 1)
                EM("vector", v_top)
                topt = sem["topd"].inc()

                W("tensor", sem["topd"], topt)

                def t_wt(t, _tc=tc):
                    t.transpose(pB[0:8, 1024:1152], wsb[:, _tc * 8:(_tc + 1) * 8],
                                ident[:]).then_inc(sem["wtr"].h, 1)
                EM("tensor", t_wt)
                wtrt = sem["wtr"].inc()

                W("vector", sem["wtr"], wtrt)

                def v_wtev(v, _tc=tc):
                    v.tensor_copy(wtt[:, _tc * 128:(_tc + 1) * 128],
                                  pB[0:8, 1024:1152]).then_inc(sem["wtev"].h, 1)
                EM("vector", v_wtev)
                sem["wtev"].inc()

            if debug and p == 0:
                W("sync", sem["wtev"], TCP)

                def s_dbg2(sy):
                    for tc_ in range(TCP):
                        sy.dma_start(dbg_x1_d[tc_ * 128:(tc_ + 1) * 128, :],
                                     x1[:, tc_, :]).then_inc(sem["dbgst"].h, 16)
                    sy.dma_start(dbg_w_d, wsb[:]).then_inc(sem["dbgst"].h, 16)
                    sy.dma_start(dbg_ln2_d, ln2t[:].rearrange("p a b -> p (a b)")).then_inc(sem["dbgst"].h, 16)
                EM("sync", s_dbg2)
                DBG2 = sem["dbgst"].inc(16 * 6)

            W("tensor", sem["wtev"], (p + 1) * TCP)
            if p == 0:
                W("tensor", sem["xld"], OH_LD)
            for e in range(E):
                W("tensor", sem["wbev"], p * E + e)

                def t_wb(t, _e=e):
                    t.matmul(pB[:, 0:512], oneh[:, _e * 128:(_e + 1) * 128], wtt[:],
                             start=True, stop=True).then_inc(sem["wbmm"].h, 1)
                EM("tensor", t_wb)
                wbt = sem["wbmm"].inc()

                W("vector", sem["wbmm"], wbt)

                def v_wb(v, _e=e):
                    i = v.tensor_copy(wball[:, _e, :], pB[:, 0:512])
                    if _e == E - 1:
                        v.drain()
                    i.then_inc(sem["wbev"].h, 1)
                EM("vector", v_wb)
                sem["wbev"].inc()

            for e in range(E):
                eg = p * E + e

                for ic in range(IC):
                    k = gu_iter(p, e, ic)
                    pair = ic % 2
                    gq = pA[:, pair * 1024:pair * 1024 + 512]
                    uq = pA[:, pair * 1024 + 512:pair * 1024 + 1024]
                    gunit = eg * NUNIT_E + (ic // 2) * 2
                    if e == 0 and ic < 2:
                        W("tensor", sem["cev"], (p + 1) * TCP)
                    W("tensor", sem["silud"], k - 1)
                    W("tensor", sem["thd"], k - 1)
                    if ic % 2 == 0:
                        W("tensor", sem["wld"], 16 * (gunit + 2))
                    loc = (ic % 2) * 2048

                    def t_gu(t, _g=gunit, _loc=loc, _gq=gq, _uq=uq):
                        gbuf = pool[_g % POOL]
                        ubuf = pool[(_g + 1) % POOL]
                        for c in range(HC):
                            i = t.matmul(_gq, gbuf[:, _loc + c * 128:_loc + (c + 1) * 128],
                                         ln2t[:, c, :], start=(c == 0), stop=(c == HC - 1))
                        i.then_inc(sem["gstop"].h, 1)
                        for c in range(HC):
                            i = t.matmul(_uq, ubuf[:, _loc + c * 128:_loc + (c + 1) * 128],
                                         ln2t[:, c, :], start=(c == 0), stop=(c == HC - 1))
                        i.then_inc(sem["ustop"].h, 1)
                    EM("tensor", t_gu)
                    gst = sem["gstop"].inc()
                    ust = sem["ustop"].inc()

                    W("scalar", sem["gstop"], gst)
                    W("scalar", sem["thd"], k - 1)

                    def s_silu(s, _gq=gq, _pair=pair):
                        s.activation(sg[_pair][:], _gq, AF.Silu).then_inc(sem["silud"].h, 1)
                    EM("scalar", s_silu)
                    sil = sem["silud"].inc()

                    W("vector", sem["ustop"], ust)
                    W("vector", sem["silud"], sil)

                    def v_hid(v, _uq=uq, _pair=pair, _ic=ic, _e=e):
                        v.tensor_tensor(th[:], sg[_pair][:], _uq, ALU.mult)
                        v.drain()
                        v.tensor_tensor(hid[:, _ic, :], th[:], wball[:, _e, :],
                                        ALU.mult).then_inc(sem["thd"].h, 1)
                    EM("vector", v_hid)
                    sem["thd"].inc()

                for hq in range(HQ):
                    for ic in range(IC):
                        dunit = eg * NUNIT_E + GU_UNITS + hq * 4 + ic // 8
                        if ic % 8 == 0:
                            W("tensor", sem["wld"], 16 * (dunit + 1))
                        W("tensor", sem["thd"], eg * IC + ic + 1)

                        def t_dn(t, _ic=ic, _hq=hq, _du=dunit, _eg=eg):
                            dbuf = pool[_du % POOL]
                            loc = (_ic % 8) * 512
                            for tc in range(TCP):
                                if _ic == 0:
                                    prev = (_eg * HQ + _hq - 1) * TCP + tc + 1
                                    if prev > 0:
                                        t.wait_ge(sem["dnev"].h, prev)
                                i = t.matmul(pB[:, tc * 512:(tc + 1) * 512],
                                             hid[:, _ic, tc * 128:(tc + 1) * 128],
                                             dbuf[:, loc:loc + 512],
                                             start=(_ic == 0), stop=(_ic == IC - 1),
                                             skip_group_check=True)
                                if _ic == IC - 1:
                                    i.then_inc(sem["dnstop"].h, 1)
                        EM("tensor", t_dn)
                    dnst = sem["dnstop"].inc(TCP)

                    for tc in range(TCP):
                        W("vector", sem["dnstop"], dnst - TCP + tc + 1)
                        if debug and p == 0 and e == 0 and hq == 0 and tc == 0:
                            W("vector", sem["dbgst"], DBG2)

                        def v_dn(v, _tc=tc, _hq=hq):
                            v.tensor_tensor(x1[:, _tc, _hq * 512:(_hq + 1) * 512],
                                            x1[:, _tc, _hq * 512:(_hq + 1) * 512],
                                            pB[:, _tc * 512:(_tc + 1) * 512],
                                            ALU.add).then_inc(sem["dnev"].h, 1)
                        EM("vector", v_dn)
                        sem["dnev"].inc()

            for tc in range(TCP):
                W("sync", sem["dnev"], dnev_after(p, E - 1, HQ - 1))

                def s_out(sy, _p=p, _tc=tc):
                    sy.dma_start(out_d[_p * 512 + _tc * 128:_p * 512 + (_tc + 1) * 128, :],
                                 x1[:, _tc, :]).then_inc(sem["outst"].h, 16)
                EM("sync", s_out)
                sem["outst"].inc(16)

        W("sync", sem["outst"], sem["outst"].n)

        with nc.Block() as blk2:
            P2.replay(blk2)

    return nc


_CACHE = {}


def _prep_shared(inputs):
    gw = np.asarray(inputs["L0_ffn_gate_w"], dtype=np.float32)
    uw = np.asarray(inputs["L0_ffn_up_w"], dtype=np.float32)
    dw = np.asarray(inputs["L0_ffn_down_w"], dtype=np.float32)
    wall = np.empty((128, E * ECOLS), dtype=np.float32)
    for e in range(E):
        base = e * ECOLS
        g = gw[e].reshape(IC, 128, HC, 128).transpose(3, 0, 2, 1).reshape(128, 65536)
        u = uw[e].reshape(IC, 128, HC, 128).transpose(3, 0, 2, 1).reshape(128, 65536)
        d = dw[e].reshape(HQ, 512, IC, 128).transpose(3, 0, 2, 1).reshape(128, 65536)
        wall[:, base:base + 65536] = g
        wall[:, base + 65536:base + 2 * 65536] = u
        wall[:, base + 2 * 65536:base + 3 * 65536] = d
    rw = np.asarray(inputs["L0_ffn_router_w"], dtype=np.float32)
    rwt = np.ascontiguousarray(rw.T.reshape(HC, 128, E).transpose(1, 0, 2).reshape(128, 128))
    return {
        "lam_ur": np.ascontiguousarray(np.asarray(
            inputs["L0_attn_Lambda_unconstrained_re"], np.float32).reshape(NH, 1)),
        "lam_im": np.ascontiguousarray(np.asarray(
            inputs["L0_attn_Lambda_im"], np.float32).reshape(NH, 1)),
        "logd": np.ascontiguousarray(np.asarray(
            inputs["L0_attn_log_Delta"], np.float32)[:NH].reshape(NH, 1)),
        "dvec": np.ascontiguousarray(np.asarray(
            inputs["L0_attn_D"], np.float32).reshape(1, D)),
        "bre": np.ascontiguousarray(np.asarray(inputs["L0_attn_B_re"], np.float32)),
        "bim": np.ascontiguousarray(np.asarray(inputs["L0_attn_B_im"], np.float32)),
        "cret": np.ascontiguousarray(np.asarray(inputs["L0_attn_C_re"], np.float32).T),
        "cimt": np.ascontiguousarray(np.asarray(inputs["L0_attn_C_im"], np.float32).T),
        "rwt": rwt,
        "wall": wall,
    }


def kernel(**inputs):
    u = np.asarray(inputs["L0_attn_u"], dtype=np.float32)
    shared = _prep_shared(inputs)
    if "nc" not in _CACHE:
        _CACHE["nc"] = build_program()
    nc = _CACHE["nc"]

    in_maps = []
    for c in range(NCORE):
        b, h = c // 2, c % 2
        x_own = np.ascontiguousarray(u[b, h * T:(h + 1) * T, :])
        x_prev = np.ascontiguousarray(u[b, 0:T, :]) if h == 1 else x_own
        m = dict(shared)
        m["x_own"] = x_own
        m["x_prev"] = x_prev
        m["cmask"] = np.full((NH, 1), float(h), dtype=np.float32)
        in_maps.append(m)

    res = run_bass_kernel_spmd(nc, in_maps, core_ids=list(range(NCORE)))
    out = np.empty((B, L, D), dtype=np.float32)
    for c in range(NCORE):
        b, h = c // 2, c % 2
        out[b, h * T:(h + 1) * T, :] = res.results[c]["out"]
    return out


# version 11
# speedup vs baseline: 1.0599x; 1.0599x over previous
import numpy as np
from contextlib import ExitStack

import concourse.bass as bass
import concourse.mybir as mybir
from concourse.bass_utils import run_bass_kernel_spmd

F32 = mybir.dt.float32
BF16 = mybir.dt.bfloat16
AF = mybir.ActivationFunctionType
ALU = mybir.AluOpType
AX = mybir.AxisListType

B, L, D = 4, 4096, 2048
NCORE = 8
T = 2048
NT = 16
NH = 32
HC = 16
E = 8
IC = 32
HQ = 4
TP = 512
NPASS = 4
TCP = 4
CH = 16
UNIT = 2048
GU_UNITS = 64
NUNIT_E = 96
NGU = 6
NDN = 4
EPS = 1e-5
INV_D = 1.0 / D
TWO_PI = 6.283185307179586
PI = 3.141592653589793
ECOLS = 3 * 65536


class Sem:
    def __init__(self, h):
        self.h = h
        self.n = 0

    def inc(self, k=1):
        self.n += k
        return self.n


class Prog:
    ENGS = ("tensor", "vector", "scalar", "gpsimd", "sync")

    def __init__(self):
        self.ops = {e: [] for e in self.ENGS}

    def em(self, eng, fn):
        self.ops[eng].append(fn)

    def wait(self, eng, sem, val):
        if val > 0:
            self.ops[eng].append(lambda e, _s=sem.h, _v=val: e.wait_ge(_s, _v))

    def replay(self, blk):
        for name in self.ENGS:
            if not self.ops[name]:
                continue
            ops = self.ops[name]

            def body(eng, _ops=ops):
                for fn in _ops:
                    fn(eng)
            getattr(blk, name)(body)


class DrainAll:

    def __init__(self, eng):
        self._e = eng

    def __getattr__(self, name):
        f = getattr(self._e, name)
        if name in ("wait_ge", "drain", "nop"):
            return f

        def wrapped(*a, **k):
            r = f(*a, **k)
            self._e.drain()
            return r
        return wrapped


def build_program(debug=False, use_wscr=False):
    nc = bass.Bass("TRN2", target_bir_lowering=False)
    if debug:
        dbg_smal_d = nc.dram_tensor("dbg_smal", [NH, 64], F32, kind="ExternalOutput").ap()
        dbg_pneg_d = nc.dram_tensor("dbg_pneg", [NH, 256], F32, kind="ExternalOutput").ap()
        dbg_ppos_d = nc.dram_tensor("dbg_ppos", [NH, 256], F32, kind="ExternalOutput").ap()
        dbg_pcar_d = nc.dram_tensor("dbg_pcar", [NH, 256], F32, kind="ExternalOutput").ap()
        dbg_bu_d = nc.dram_tensor("dbg_bu", [NH, 2 * T], F32, kind="ExternalOutput").ap()
        dbg_xs_d = nc.dram_tensor("dbg_xs", [NH, 2 * T], F32, kind="ExternalOutput").ap()
        dbg_x1_d = nc.dram_tensor("dbg_x1", [TP, D], F32, kind="ExternalOutput").ap()
        dbg_w_d = nc.dram_tensor("dbg_w", [128, TCP * 8], F32, kind="ExternalOutput").ap()
        dbg_ln2_d = nc.dram_tensor("dbg_ln2", [128, HC * TP], BF16, kind="ExternalOutput").ap()

    x_own_d = nc.dram_tensor("x_own", [T, D], F32, kind="ExternalInput").ap()
    x_prev_d = nc.dram_tensor("x_prev", [T, D], F32, kind="ExternalInput").ap()
    cmask_d = nc.dram_tensor("cmask", [NH, 1], F32, kind="ExternalInput").ap()
    lam_ur_d = nc.dram_tensor("lam_ur", [NH, 1], F32, kind="ExternalInput").ap()
    lam_im_d = nc.dram_tensor("lam_im", [NH, 1], F32, kind="ExternalInput").ap()
    logd_d = nc.dram_tensor("logd", [NH, 1], F32, kind="ExternalInput").ap()
    dvec_d = nc.dram_tensor("dvec", [1, D], F32, kind="ExternalInput").ap()
    bre_d = nc.dram_tensor("bre", [NH, D], F32, kind="ExternalInput").ap()
    bim_d = nc.dram_tensor("bim", [NH, D], F32, kind="ExternalInput").ap()
    cret_d = nc.dram_tensor("cret", [NH, D], F32, kind="ExternalInput").ap()
    cimt_d = nc.dram_tensor("cimt", [NH, D], F32, kind="ExternalInput").ap()
    rwt_d = nc.dram_tensor("rwt", [128, 128], F32, kind="ExternalInput").ap()
    wall_d = nc.dram_tensor("wall", [128, E * ECOLS], F32, kind="ExternalInput").ap()
    out_d = nc.dram_tensor("out", [T, D], F32, kind="ExternalOutput").ap()
    du_d = nc.dram_tensor("du_scr", [T, D], BF16, kind="Internal").ap()
    _wscr = [nc.dram_tensor(f"w_scr{i}", [128, E * NUNIT_E * UNIT // 2], BF16,
                            kind="Internal").ap() for i in range(2)]

    def wscr_slice(j):
        half = E * NUNIT_E // 2
        t = _wscr[j // half]
        jj = j % half
        return t[:, jj * UNIT:(jj + 1) * UNIT]

    with ExitStack() as S:
        sem = {}
        for name in ("wg0", "wg1", "wg2", "wg3", "wg4", "wg5",
                     "wd0", "wd1", "wd2", "wd3",
                     "dud", "dul", "wst", "dnu", "dbgst", "xld", "gld", "gsu", "sqd", "rstd", "ln1d", "trab", "evab", "bustop",
                     "bufree", "sv", "vv", "tt", "scand", "abdone",
                     "wld", "x2ld", "ymm", "yev", "ctr", "ltr", "cev",
                     "rmm", "rev", "topd", "wtr", "wtev",
                     "wbmm", "wbev", "gstop", "ustop", "silud", "thd",
                     "dnstop", "dnev", "outst"):
            sem[name] = Sem(S.enter_context(nc.semaphore(name)))

        def sb(name, shape, dt):
            return S.enter_context(nc.sbuf_tensor(name, shape, dt))

        xt = [sb(f"xt{i}", [128, D], F32) for i in range(2)]
        x1 = sb("x1", [128, TCP, D], F32)
        ln2t = sb("ln2t", [128, HC, TP], BF16)
        sg = [sb(f"sg{i}", [128, TP], F32) for i in range(2)]
        th = sb("th", [128, TP], F32)
        dbc = sb("dbc", [128, D], F32)
        xsbf = sb("xsbf", [NH, 2 * T], BF16)
        c2re = sb("c2re", [NH, D], BF16)
        c2im = sb("c2im", [NH, D], BF16)
        rwtb = sb("rwtb", [128, 128], BF16)
        bbtre = sb("bbtre", [128, HC * NH], BF16)
        bbtim = sb("bbtim", [128, HC * NH], BF16)
        ident = sb("ident", [128, 128], F32)
        ltbig = sb("ltbig", [128, 64], F32)
        oneh = sb("oneh", [8, E * 128], F32)
        onesr = sb("onesr", [1, 128], F32)
        onescan = sb("onescan", [NH, 128], F32)
        pneg = sb("pneg", [NH, 256], F32)
        ppos = sb("ppos", [NH, 256], F32)
        pcar = sb("pcar", [NH, 256], F32)
        smal = sb("smal", [NH, 64], F32)
        stat = sb("stat", [128, 8], F32)
        wsb = sb("wsb", [128, TCP * 8], F32)
        wtt = sb("wtt", [8, TP], F32)
        rfm = sb("rfm", [8, 128], F32)
        ltm = sb("ltm", [128, 8], F32)
        prob = sb("prob", [128, 8], F32)
        ptmp = sb("ptmp", [128, 64], F32)
        ptmp2 = sb("ptmp2", [128, 64], F32)
        pS = sb("pS", [128, 8], F32)

        SC_NAMES = ("lam_ur", "lam_im", "logd", "t1", "t2", "t3", "t4", "sp",
                    "lam_re", "delta", "ldre", "ldim", "mag", "m1", "g1",
                    "m2c", "g2", "m3", "s2", "s2sq", "cosv", "sinv", "lbre",
                    "lbim", "a1", "u1", "u2", "nre", "nim", "den", "rden",
                    "cofre", "cofim", "r2", "ir2", "invre", "invim",
                    "pwre", "pwim", "npre", "npim",
                    "l128re", "l128im", "cmask", "ccre", "ccim",
                    "tca", "tcb", "tcc", "carry_re", "carry_im")
        SC = {n: i for i, n in enumerate(SC_NAMES)}

        def sc(n):
            return smal[:, SC[n]:SC[n] + 1]

        pA = S.enter_context(nc.psum_tensor("pA", [128, 2048], F32))
        pB = S.enter_context(nc.psum_tensor("pB", [128, 2048], F32))

        P = Prog()
        xld, vv, sv, tt = sem["xld"], sem["vv"], sem["sv"], sem["tt"]

        def g_setup(g):
            g.memset(ident[:], 0.0)
            g.drain()
            g.affine_select(out=ident[:], in_=ident[:], compare_op=ALU.not_equal,
                            fill=1.0, base=0, pattern=[[-1, 128]], channel_multiplier=1)
            g.memset(ltbig[:], 0.0)
            for e in range(1, E):
                g.memset(ltbig[:, e * 8:e * 8 + e], 1.0)
            g.memset(oneh[:], 0.0).then_inc(sem["gsu"].h, 1)
        P.em("gpsimd", g_setup)
        GSU = sem["gsu"].inc()

        def s_smalls(sy):
            for dst, src in ((sc("lam_ur"), lam_ur_d), (sc("lam_im"), lam_im_d),
                             (sc("logd"), logd_d), (sc("cmask"), cmask_d),
                             (dbc[0:1, :], dvec_d)):
                sy.dma_start(dst, src).then_inc(xld.h, 16)
        P.em("sync", s_smalls)
        SMALLS = xld.inc(16 * 5)

        P.wait("sync", sem["gsu"], GSU)
        P.wait("sync", vv, 1)

        def s_oneh(sy):
            for e in range(E):
                sy.dma_start(oneh[e:e + 1, e * 128:(e + 1) * 128],
                             onesr[0:1, 0:128]).then_inc(xld.h, 16)
        P.em("sync", s_oneh)
        OH_LD = xld.inc(16 * 8)

        def g_cast_loads(g):
            g.dma_start(c2re[:], cret_d).then_inc(sem["gld"].h, 16)
            g.dma_start(c2im[:], cimt_d).then_inc(sem["gld"].h, 16)
            g.dma_start(rwtb[:], rwt_d).then_inc(sem["gld"].h, 16)
        P.em("gpsimd", g_cast_loads)
        GLD = sem["gld"].inc(16 * 3)

        def v_consts(v):
            v.memset(onesr[:], 1.0)
            v.memset(onescan[:], 1.0).then_inc(vv.h, 1)
        P.em("vector", v_consts)
        V_CONST = vv.inc()

        P.wait("scalar", xld, SMALLS)

        def s_set1(s):
            s.activation(sc("t1"), sc("lam_ur"), AF.Abs)
            s.drain()
            s.activation(sc("t2"), sc("t1"), AF.Exp, scale=-1.0)
            s.drain()
            s.activation(sc("t3"), sc("t2"), AF.Ln, bias=1.0)
            s.activation(sc("t4"), sc("lam_ur"), AF.Relu)
            s.activation(sc("delta"), sc("logd"), AF.Exp).then_inc(sv.h, 1)
        P.em("scalar", s_set1)
        S_SET1 = sv.inc()

        P.wait("vector", sv, S_SET1)

        def v_set1(v):
            v.tensor_tensor(sc("sp"), sc("t4"), sc("t3"), ALU.add)
            v.drain()
            v.tensor_scalar(sc("lam_re"), sc("sp"), 0.0101, -1.0, ALU.add, ALU.mult)
            v.drain()
            v.tensor_tensor(sc("ldre"), sc("lam_re"), sc("delta"), ALU.mult)
            v.tensor_tensor(sc("ldim"), sc("lam_im"), sc("delta"), ALU.mult)
            MAGIC = 12582912.0
            v.drain()
            v.tensor_single_scalar(sc("m1"), sc("ldim"), 1.0 / TWO_PI, ALU.mult)
            v.drain()
            v.tensor_single_scalar(sc("g1"), sc("m1"), MAGIC, ALU.add)
            v.drain()
            v.tensor_single_scalar(sc("g2"), sc("g1"), MAGIC, ALU.subtract)
            v.drain()
            v.tensor_single_scalar(sc("m2c"), sc("g2"), -TWO_PI, ALU.mult)
            v.drain()
            v.tensor_tensor(sc("m3"), sc("ldim"), sc("m2c"), ALU.add).then_inc(vv.h, 1)
        P.em("vector", v_set1)
        V_SET1 = vv.inc()

        P.wait("scalar", vv, V_SET1)

        def s_set2(s):
            s.activation(sc("mag"), sc("ldre"), AF.Exp)
            s.activation(sc("s2"), sc("m3"), AF.Sin, scale=0.5)
            s.activation(sc("sinv"), sc("m3"), AF.Sin).then_inc(sv.h, 1)
        P.em("scalar", s_set2)
        S_SET2 = sv.inc()

        P.wait("vector", sv, S_SET2)
        P.wait("vector", xld, SMALLS)

        def v_set2(v):
            v = DrainAll(v)
            v.tensor_tensor(sc("s2sq"), sc("s2"), sc("s2"), ALU.mult)
            v.tensor_scalar(sc("cosv"), sc("s2sq"), -2.0, 1.0, ALU.mult, ALU.add)
            v.tensor_tensor(sc("lbre"), sc("mag"), sc("cosv"), ALU.mult)
            v.tensor_tensor(sc("lbim"), sc("mag"), sc("sinv"), ALU.mult)
            v.tensor_single_scalar(sc("a1"), sc("lbre"), 1.0, ALU.subtract)
            v.tensor_tensor(sc("u1"), sc("a1"), sc("lam_re"), ALU.mult)
            v.tensor_tensor(sc("u2"), sc("lbim"), sc("lam_im"), ALU.mult)
            v.tensor_tensor(sc("nre"), sc("u1"), sc("u2"), ALU.add)
            v.tensor_tensor(sc("u1"), sc("lbim"), sc("lam_re"), ALU.mult)
            v.tensor_tensor(sc("u2"), sc("a1"), sc("lam_im"), ALU.mult)
            v.tensor_tensor(sc("nim"), sc("u1"), sc("u2"), ALU.subtract)
            v.tensor_tensor(sc("u1"), sc("lam_re"), sc("lam_re"), ALU.mult)
            v.tensor_tensor(sc("u2"), sc("lam_im"), sc("lam_im"), ALU.mult)
            v.tensor_tensor(sc("den"), sc("u1"), sc("u2"), ALU.add)
            v.reciprocal(sc("rden"), sc("den"))
            v.tensor_tensor(sc("cofre"), sc("nre"), sc("rden"), ALU.mult)
            v.tensor_tensor(sc("cofim"), sc("nim"), sc("rden"), ALU.mult)
            v.tensor_tensor(sc("u1"), sc("lbre"), sc("lbre"), ALU.mult)
            v.tensor_tensor(sc("u2"), sc("lbim"), sc("lbim"), ALU.mult)
            v.tensor_tensor(sc("r2"), sc("u1"), sc("u2"), ALU.add)
            v.reciprocal(sc("ir2"), sc("r2"))
            v.tensor_tensor(sc("invre"), sc("lbre"), sc("ir2"), ALU.mult)
            v.tensor_tensor(sc("invim"), sc("lbim"), sc("ir2"), ALU.mult)
            v.tensor_single_scalar(sc("invim"), sc("invim"), -1.0, ALU.mult).then_inc(vv.h, 1)
        P.em("vector", v_set2)
        V_SET2 = vv.inc()

        P.wait("tensor", vv, V_CONST)
        P.wait("tensor", xld, SMALLS)

        def t_dbc(t):
            for j in range(4):
                i = t.matmul(pA[:, j * 512:(j + 1) * 512], onesr[:],
                             dbc[0:1, j * 512:(j + 1) * 512], start=True, stop=True)
            i.then_inc(tt.h, 1)
        P.em("tensor", t_dbc)
        T_DBC = tt.inc()

        P.wait("vector", tt, T_DBC)

        def v_dbc(v):
            v.tensor_copy(dbc[:], pA[:]).then_inc(vv.h, 1)
        P.em("vector", v_dbc)
        V_DBC = vv.inc()

        with ExitStack() as S_ab:
            def sba(name, shape, dt):
                return S_ab.enter_context(nc.sbuf_tensor(name, shape, dt))
            bre_sb = sba("bre_sb", [NH, D], F32)
            bim_sb = sba("bim_sb", [NH, D], F32)
            ln1b = [sba(f"ln1b{i}", [128, D], F32) for i in range(2)]
            lntb = [sba(f"lntb{i}", [128, D], BF16) for i in range(2)]
            busb = sba("busb", [NH, 2 * T], F32)
            usb = sba("usb", [NH, 2 * T], F32)
            cssb = sba("cssb", [NH, 2 * T], F32)
            qts = sba("qts", [NH, 256], F32)
            carl = sba("carl", [NH, 96], F32)
            xsqb = x1[:, 0, :]
            bbre = cssb[:, 0:D]
            bbim = cssb[:, D:2 * D]

            P.wait("vector", sv, S_SET2)

            def v_tables(v):
                v = DrainAll(v)
                def build_table(tab, base_re, base_im):
                    v.memset(tab[:, 0:1], 1.0)
                    v.memset(tab[:, 128:129], 0.0)
                    v.tensor_copy(sc("pwre"), base_re)
                    v.tensor_copy(sc("pwim"), base_im)
                    ln_ = 1
                    while ln_ < 128:
                        re0, im0 = tab[:, 0:ln_], tab[:, 128:128 + ln_]
                        re1, im1 = tab[:, ln_:2 * ln_], tab[:, 128 + ln_:128 + 2 * ln_]
                        ta, tb = qts[:, 0:ln_], qts[:, 128:128 + ln_]
                        v.tensor_scalar(ta, re0, sc("pwre"), None, ALU.mult)
                        v.tensor_scalar(tb, im0, sc("pwim"), None, ALU.mult)
                        v.tensor_tensor(re1, ta, tb, ALU.subtract)
                        v.tensor_scalar(ta, re0, sc("pwim"), None, ALU.mult)
                        v.tensor_scalar(tb, im0, sc("pwre"), None, ALU.mult)
                        v.tensor_tensor(im1, ta, tb, ALU.add)
                        if 2 * ln_ < 128:
                            v.tensor_tensor(sc("npre"), sc("pwre"), sc("pwre"), ALU.mult)
                            v.tensor_tensor(sc("npim"), sc("pwim"), sc("pwim"), ALU.mult)
                            v.tensor_tensor(sc("npre"), sc("npre"), sc("npim"), ALU.subtract)
                            v.tensor_tensor(sc("npim"), sc("pwre"), sc("pwim"), ALU.mult)
                            v.tensor_single_scalar(sc("npim"), sc("npim"), 2.0, ALU.mult)
                            v.tensor_copy(sc("pwre"), sc("npre"))
                            v.tensor_copy(sc("pwim"), sc("npim"))
                        ln_ *= 2
                build_table(pneg, sc("invre"), sc("invim"))
                build_table(ppos, sc("lbre"), sc("lbim"))
                ta, tb = qts[:, 0:128], qts[:, 128:256]
                v.tensor_scalar(ta, ppos[:, 0:128], sc("lbre"), None, ALU.mult)
                v.tensor_scalar(tb, ppos[:, 128:256], sc("lbim"), None, ALU.mult)
                v.tensor_tensor(pcar[:, 0:128], ta, tb, ALU.subtract)
                v.tensor_scalar(ta, ppos[:, 0:128], sc("lbim"), None, ALU.mult)
                v.tensor_scalar(tb, ppos[:, 128:256], sc("lbre"), None, ALU.mult)
                v.tensor_tensor(pcar[:, 128:256], ta, tb, ALU.add)
                v.tensor_tensor(sc("tca"), ppos[:, 127:128], sc("lbre"), ALU.mult)
                v.tensor_tensor(sc("tcb"), ppos[:, 255:256], sc("lbim"), ALU.mult)
                v.tensor_tensor(sc("l128re"), sc("tca"), sc("tcb"), ALU.subtract)
                v.tensor_tensor(sc("tca"), ppos[:, 127:128], sc("lbim"), ALU.mult)
                v.tensor_tensor(sc("tcb"), ppos[:, 255:256], sc("lbre"), ALU.mult)
                v.tensor_tensor(sc("l128im"), sc("tca"), sc("tcb"), ALU.add).then_inc(vv.h, 1)
            P.em("vector", v_tables)
            V_TABLES = vv.inc()

            def s_bload(sy):
                sy.dma_start(bre_sb[:], bre_d).then_inc(xld.h, 16)
                sy.dma_start(bim_sb[:], bim_d).then_inc(xld.h, 16)
            P.em("sync", s_bload)
            BLD = xld.inc(32)

            P.wait("vector", xld, BLD)

            def v_bbar(v):
                v = DrainAll(v)
                w1, w2 = busb[:, 0:D], busb[:, D:2 * D]
                v.tensor_scalar(w1, bre_sb[:], sc("cofre"), None, ALU.mult)
                v.tensor_scalar(w2, bim_sb[:], sc("cofim"), None, ALU.mult)
                v.tensor_tensor(bbre, w1, w2, ALU.subtract)
                v.tensor_scalar(w1, bre_sb[:], sc("cofim"), None, ALU.mult)
                v.tensor_scalar(w2, bim_sb[:], sc("cofre"), None, ALU.mult)
                v.tensor_tensor(bbim, w1, w2, ALU.add).then_inc(vv.h, 1)
            P.em("vector", v_bbar)
            V_BBAR = vv.inc()

            P.wait("tensor", vv, V_BBAR)

            def t_bbt(t):
                for c in range(HC):
                    t.transpose(pA[:, c * 32:(c + 1) * 32],
                                cssb[:, c * 128:(c + 1) * 128], ident[0:32, 0:32])
                for c in range(HC):
                    i = t.transpose(pA[:, 512 + c * 32:512 + (c + 1) * 32],
                                    cssb[:, D + c * 128:D + (c + 1) * 128], ident[0:32, 0:32])
                i.then_inc(tt.h, 1)
            P.em("tensor", t_bbt)
            T_BBT = tt.inc()

            P.wait("vector", tt, T_BBT)

            def v_bbt(v):
                v.tensor_copy(bbtre[:], pA[:, 0:512])
                v.tensor_copy(bbtim[:], pA[:, 512:1024]).then_inc(vv.h, 1)
            P.em("vector", v_bbt)
            V_BBT = vv.inc()

            dub = [sba(f"dub{i}", [128, D], BF16) for i in range(2)]

            def stc(base, par):
                return stat[:, base + 4 * par:base + 4 * par + 1]

            for ti in range(32):
                src_d = x_prev_d if ti < 16 else x_own_d
                row = (ti % 16) * 128
                buf = ti % 2
                stm = stc(0, buf)
                stvar = stc(1, buf)
                strst = stc(2, buf)
                stsq = stc(3, buf)

                P.wait("sync", sem["ln1d"], ti - 1)
                tgt = xld.inc(16)

                def s_ld(sy, _s=src_d, _r=row, _b=buf):
                    sy.dma_start(xt[_b][:], _s[_r:_r + 128, :]).then_inc(xld.h, 16)
                P.em("sync", s_ld)

                P.wait("scalar", xld, tgt)
                P.wait("scalar", sem["scand"], ti - 1)

                def s_sq(s, _b=buf, _sq=stsq):
                    s.activation(xsqb, xt[_b][:], AF.Square,
                                 accum_out=_sq).then_inc(sem["sqd"].h, 1)
                P.em("scalar", s_sq)
                sqt = sem["sqd"].inc()

                P.wait("vector", xld, tgt)
                P.wait("vector", sem["sqd"], sqt)

                def v_stats(v, _b=buf, _m=stm, _v=stvar, _sq=stsq):
                    v = DrainAll(v)
                    v.reduce_sum(_m[:], xt[_b][:], axis=AX.X)
                    v.tensor_single_scalar(_m[:], _m[:], INV_D, ALU.mult)
                    v.tensor_scalar(_v[:], _sq[:], INV_D, EPS, ALU.mult, ALU.add)
                    v.scalar_tensor_tensor(_v[:], _m[:], 1.0, _v[:], ALU.mult,
                                           ALU.subtract) if False else None
                    v.tensor_tensor(ltm[:, 0:1], _m[:], _m[:], ALU.mult)
                    v.tensor_tensor(_v[:], _v[:], ltm[:, 0:1], ALU.subtract).then_inc(sem["scand"].h, 1)
                P.em("vector", v_stats)
                vstat = sem["scand"].inc()

                P.wait("scalar", sem["scand"], vstat)

                def s_rstd(s, _v=stvar, _r=strst):
                    s.activation(_r[:], _v[:], AF.Ln)
                    s.drain()
                    s.activation(_r[:], _r[:], AF.Exp, scale=-0.5).then_inc(sem["rstd"].h, 1)
                P.em("scalar", s_rstd)
                rst = sem["rstd"].inc()

                P.wait("vector", sem["rstd"], rst)
                nmb = ltm[:, 1 + buf:2 + buf]

                def v_nmb(v, _m=stm, _r=strst, _n=nmb):
                    v.tensor_scalar(_n, _m[:], -1.0, _r[:], ALU.mult, ALU.mult)
                    v.drain()
                P.em("vector", v_nmb)
                nmbt = sem["scand"].inc()
                P.em("vector", lambda v, _t=nmbt: None)
                P.ops["vector"].pop()
                P.ops["vector"].pop()

                def v_nmb2(v, _m=stm, _r=strst, _n=nmb):
                    i = v.tensor_scalar(_n, _m[:], -1.0, _r[:], ALU.mult, ALU.mult)
                    v.drain()
                    i.then_inc(sem["scand"].h, 1)
                P.em("vector", v_nmb2)

                P.wait("scalar", sem["scand"], nmbt)
                P.wait("scalar", sem["trab"], 2 * (ti - 1))
                if ti >= 18:
                    P.wait("scalar", sem["dud"], ti - 17)

                def s_apply(s, _b=buf, _r=strst, _n=nmb):
                    s.activation(ln1b[_b][:], xt[_b][:], AF.Identity, scale=_r[:],
                                 bias=_n).then_inc(sem["ln1d"].h, 1)
                P.em("scalar", s_apply)
                ln1t = sem["ln1d"].inc()

                if ti >= 16:
                    P.wait("vector", sem["ln1d"], ln1t)
                    if ti >= 18:
                        P.wait("vector", sem["dul"], 16 * (ti - 17))

                    def v_du(v, _b=buf):
                        v.tensor_tensor(dub[_b][:], ln1b[_b][:], dbc[:],
                                        ALU.mult).then_inc(sem["dud"].h, 1)
                    P.em("vector", v_du)
                    dut_ = sem["dud"].inc()
                    P.wait("sync", sem["dud"], dut_)

                    def s_dust(sy, _b=buf, _r=(ti - 16) * 128):
                        sy.dma_start(du_d[_r:_r + 128, :], dub[_b][:]).then_inc(sem["dul"].h, 16)
                    P.em("sync", s_dust)
                    sem["dul"].inc(16)

                P.wait("tensor", sem["ln1d"], ln1t)
                if ti == 0:
                    P.wait("tensor", vv, V_BBT)
                else:
                    P.wait("tensor", sem["evab"], 2 * ti)

                def t_tr(t, _b=buf):
                    for c in range(8):
                        i1 = t.transpose(pA[:, c * 128:(c + 1) * 128],
                                         ln1b[_b][:, c * 128:(c + 1) * 128], ident[:])
                    i1.then_inc(sem["trab"].h, 1)
                    for c in range(8, 16):
                        i2 = t.transpose(pA[:, c * 128:(c + 1) * 128],
                                         ln1b[_b][:, c * 128:(c + 1) * 128], ident[:])
                    i2.then_inc(sem["trab"].h, 1)
                P.em("tensor", t_tr)
                trt = sem["trab"].inc(2)

                P.wait("scalar", sem["bustop"], 2 * (ti - 1))

                def s_ev(s, _b=buf, _t=trt):
                    s.wait_ge(sem["trab"].h, _t - 1)
                    s.copy(lntb[_b][:, 0:1024], pA[:, 0:1024]).then_inc(sem["evab"].h, 1)
                    s.wait_ge(sem["trab"].h, _t)
                    s.copy(lntb[_b][:, 1024:2048], pA[:, 1024:2048]).then_inc(sem["evab"].h, 1)
                P.em("scalar", s_ev)
                evt = sem["evab"].inc(2)

                P.wait("tensor", sem["evab"], evt)
                P.wait("tensor", sem["bufree"], ti)

                def t_bu(t, _b=buf):
                    for c in range(HC):
                        t.matmul(pB[0:32, 0:128], bbtre[:, c * 32:(c + 1) * 32],
                                 lntb[_b][:, c * 128:(c + 1) * 128],
                                 start=(c == 0), stop=(c == HC - 1))
                    for c in range(HC):
                        i = t.matmul(pB[0:32, 512:640], bbtim[:, c * 32:(c + 1) * 32],
                                     lntb[_b][:, c * 128:(c + 1) * 128],
                                     start=(c == 0), stop=(c == HC - 1))
                    i.then_inc(sem["bustop"].h, 2)
                P.em("tensor", t_bu)
                but = sem["bustop"].inc(2)

                tcol = (ti % 16) * 128
                P.wait("vector", sem["bustop"], but)

                def v_bu(v, _tc=tcol):
                    v.tensor_copy(busb[:, _tc:_tc + 128], pB[0:32, 0:128])
                    v.tensor_copy(busb[:, T + _tc:T + _tc + 128],
                                  pB[0:32, 512:640]).then_inc(sem["bufree"].h, 1)
                P.em("vector", v_bu)
                sem["bufree"].inc()

                if ti in (15, 31):
                    own = ti == 31

                    def v_scan(v, _own=own):
                        v = DrainAll(v)
                        bre, bim = busb[:, 0:T], busb[:, T:2 * T]
                        ure, uim = usb[:, 0:T], usb[:, T:2 * T]
                        csre, csim = cssb[:, 0:T], cssb[:, T:2 * T]
                        pnre, pnim = pneg[:, 0:128], pneg[:, 128:256]
                        ta, tb = qts[:, 0:128], qts[:, 128:256]
                        for c in range(CH):
                            s0, s1 = c * 128, (c + 1) * 128
                            v.tensor_tensor(ta, bre[:, s0:s1], pnre, ALU.mult)
                            v.tensor_tensor(tb, bim[:, s0:s1], pnim, ALU.mult)
                            v.tensor_tensor(ure[:, s0:s1], ta, tb, ALU.subtract)
                            v.tensor_tensor(ta, bre[:, s0:s1], pnim, ALU.mult)
                            v.tensor_tensor(tb, bim[:, s0:s1], pnre, ALU.mult)
                            v.tensor_tensor(uim[:, s0:s1], ta, tb, ALU.add)
                        for c in range(CH):
                            s0, s1 = c * 128, (c + 1) * 128
                            v.tensor_tensor_scan(csre[:, s0:s1], onescan[:], ure[:, s0:s1],
                                                 0.0, ALU.mult, ALU.add)
                            v.tensor_tensor_scan(csim[:, s0:s1], onescan[:], uim[:, s0:s1],
                                                 0.0, ALU.mult, ALU.add)
                        cre_s = cssb[:, 127:T:128]
                        cim_s = cssb[:, T + 127:2 * T:128]
                        car_re, car_im = carl[:, 0:16], carl[:, 16:32]
                        t16a, t16b = carl[:, 32:48], carl[:, 48:64]
                        v.tensor_scalar(t16a, cre_s, ppos[:, 127:128], None, ALU.mult)
                        v.tensor_scalar(t16b, cim_s, ppos[:, 255:256], None, ALU.mult)
                        v.tensor_tensor(car_re, t16a, t16b, ALU.subtract)
                        v.tensor_scalar(t16a, cre_s, ppos[:, 255:256], None, ALU.mult)
                        v.tensor_scalar(t16b, cim_s, ppos[:, 127:128], None, ALU.mult)
                        v.tensor_tensor(car_im, t16a, t16b, ALU.add)
                        if not _own:
                            v.memset(sc("ccre"), 0.0)
                            v.memset(sc("ccim"), 0.0)
                        else:
                            v.tensor_tensor(sc("ccre"), sc("carry_re"), sc("cmask"), ALU.mult)
                            v.tensor_tensor(sc("ccim"), sc("carry_im"), sc("cmask"), ALU.mult)
                        ccm_re, ccm_im = carl[:, 64:80], carl[:, 80:96]
                        for c in range(CH):
                            v.tensor_copy(ccm_re[:, c:c + 1], sc("ccre"))
                            v.tensor_copy(ccm_im[:, c:c + 1], sc("ccim"))
                            v.tensor_tensor(sc("tca"), sc("ccre"), sc("l128re"), ALU.mult)
                            v.tensor_tensor(sc("tcb"), sc("ccim"), sc("l128im"), ALU.mult)
                            v.tensor_tensor(sc("tcc"), sc("tca"), sc("tcb"), ALU.subtract)
                            v.tensor_tensor(sc("tca"), sc("ccre"), sc("l128im"), ALU.mult)
                            v.tensor_tensor(sc("tcb"), sc("ccim"), sc("l128re"), ALU.mult)
                            v.tensor_tensor(sc("ccim"), sc("tca"), sc("tcb"), ALU.add)
                            v.tensor_tensor(sc("ccim"), sc("ccim"), car_im[:, c:c + 1], ALU.add)
                            v.tensor_tensor(sc("ccre"), sc("tcc"), car_re[:, c:c + 1], ALU.add)
                        if not _own:
                            v.tensor_copy(sc("carry_re"), sc("ccre"))
                            v.tensor_copy(sc("carry_im"), sc("ccim"))
                            return
                        ppre, ppim = ppos[:, 0:128], ppos[:, 128:256]
                        pcre, pcim = pcar[:, 0:128], pcar[:, 128:256]
                        for c in range(CH):
                            s0, s1 = c * 128, (c + 1) * 128
                            v.tensor_tensor(ta, csre[:, s0:s1], ppre, ALU.mult)
                            v.tensor_tensor(tb, csim[:, s0:s1], ppim, ALU.mult)
                            v.tensor_tensor(ure[:, s0:s1], ta, tb, ALU.subtract)
                            v.tensor_tensor(ta, csre[:, s0:s1], ppim, ALU.mult)
                            v.tensor_tensor(tb, csim[:, s0:s1], ppre, ALU.mult)
                            v.tensor_tensor(uim[:, s0:s1], ta, tb, ALU.add)
                            v.tensor_scalar(ta, pcre, ccm_re[:, c:c + 1], None, ALU.mult)
                            v.tensor_scalar(tb, pcim, ccm_im[:, c:c + 1], None, ALU.mult)
                            v.tensor_tensor(ta, ta, tb, ALU.subtract)
                            v.tensor_tensor(ure[:, s0:s1], ure[:, s0:s1], ta, ALU.add)
                            v.tensor_scalar(ta, pcre, ccm_im[:, c:c + 1], None, ALU.mult)
                            v.tensor_scalar(tb, pcim, ccm_re[:, c:c + 1], None, ALU.mult)
                            v.tensor_tensor(ta, ta, tb, ALU.add)
                            v.tensor_tensor(uim[:, s0:s1], uim[:, s0:s1], ta, ALU.add)
                        v.tensor_single_scalar(xsbf[:, 0:T], usb[:, 0:T], 2.0, ALU.mult)
                        v.tensor_single_scalar(xsbf[:, T:2 * T], usb[:, T:2 * T],
                                               -2.0, ALU.mult).then_inc(sem["abdone"].h, 1)
                    P.em("vector", v_scan)
                    if own:
                        sem["abdone"].inc()

            if debug:
                P.wait("sync", sem["abdone"], 1)

                def s_dbg1(sy):
                    sy.dma_start(dbg_smal_d, smal[:]).then_inc(sem["dbgst"].h, 16)
                    sy.dma_start(dbg_pneg_d, pneg[:]).then_inc(sem["dbgst"].h, 16)
                    sy.dma_start(dbg_ppos_d, ppos[:]).then_inc(sem["dbgst"].h, 16)
                    sy.dma_start(dbg_pcar_d, pcar[:]).then_inc(sem["dbgst"].h, 16)
                    sy.dma_start(dbg_bu_d, busb[:]).then_inc(sem["dbgst"].h, 16)
                    sy.dma_start(dbg_xs_d, usb[:]).then_inc(sem["dbgst"].h, 16)
                    sy.wait_ge(sem["dbgst"].h, 16 * 6)
                P.em("sync", s_dbg1)
                sem["dbgst"].inc(16 * 6)
            with nc.Block() as blk1:
                P.replay(blk1)

        gupool = [sb(f"wg{i}", [128, UNIT], BF16) for i in range(NGU)]
        dnpool = [sb(f"wd{i}", [128, UNIT], BF16) for i in range(NDN)]
        wball = sb("wball", [128, E, TP], BF16)
        dut = [sb(f"dut{i}", [128, D], BF16) for i in range(2)]
        hid = sb("hid", [128, IC, TP], BF16)
        lnc = sb("lnc", [128, D], F32)
        xsqc = hid[:, 0:4, :].rearrange("p a b -> p (a b)")

        P2 = Prog()
        W = P2.wait
        EM = P2.em

        def gu_iter(p, e, ic):
            return ((p * E) + e) * IC + ic

        def dnev_after(p, e, hq):
            return (((p * E) + e) * HQ + hq + 1) * TCP

        NUNIT_P = E * NUNIT_E

        def is_gu(k):
            return k < GU_UNITS

        def unit_free(p, e, k):
            if is_gu(k):
                return ("thd", gu_iter(p, e, k // 2) + 1)
            hq, icg = divmod(k - GU_UNITS, 8)
            eg = p * E + e
            if icg < 7:
                return ("dnu", (eg * HQ + hq) * 7 + icg + 1)
            return ("dnstop", eg * 16 + (hq + 1) * 4)

        issue_list = [(p, e, k) for p in range(NPASS) for e in range(E)
                      for k in range(NUNIT_E)]
        wld_pos = {}
        slot_of = {}
        slot_sem = {}
        prior_in_slot = {}
        gu_n = dn_n = 0
        gu_hist, dn_hist = [], []
        slot_inst = {}
        for i, u in enumerate(issue_list):
            wld_pos[u] = i
            if is_gu(u[2]):
                s = gu_n % NGU
                slot_of[u] = ("gu", s)
                prior_in_slot[u] = gu_hist[gu_n - NGU] if gu_n >= NGU else None
                gu_hist.append(u)
                gu_n += 1
                key = ("gu", s)
            else:
                s = dn_n % NDN
                slot_of[u] = ("dn", s)
                prior_in_slot[u] = dn_hist[dn_n - NDN] if dn_n >= NDN else None
                dn_hist.append(u)
                dn_n += 1
                key = ("dn", s)
            inst = slot_inst.get(key, 0)
            slot_inst[key] = inst + 1
            nm = (f"wg{key[1]}" if key[0] == "gu" else f"wd{key[1]}")
            slot_sem[u] = (nm, 16 * (inst + 1))

        def buf_of(u):
            kind, s = slot_of[u]
            return gupool[s] if kind == "gu" else dnpool[s]

        def wb_pass(j):
            if not use_wscr:
                return NPASS + 1
            return 0 if j % 2 == 0 else 1

        store_seq = [j for j in range(NUNIT_P) if wb_pass(j) == 0] + \
                    [j for j in range(NUNIT_P) if wb_pass(j) == 1]
        store_idx = {}
        for i_, j in enumerate(store_seq):
            store_idx[(wb_pass(j), j)] = i_

        def unit_cols(e, k):
            base = e * ECOLS
            if is_gu(k):
                return base + (0 if k % 2 == 0 else 65536) + (k // 2) * UNIT
            return base + 2 * 65536 + (k - GU_UNITS) * UNIT

        def g_stream(g):
            g.wait_ge(sem["abdone"].h, 1)
            if debug:
                g.wait_ge(sem["dbgst"].h, 16 * 6)
            for n, u in enumerate(issue_list):
                p, e, k = u
                j = e * NUNIT_E + k
                pr = prior_in_slot[u]
                if pr is not None:
                    sname, val = unit_free(*pr)
                    g.wait_ge(sem[sname].h, val)
                    jp = pr[1] * NUNIT_E + pr[2]
                    if use_wscr and pr[0] == wb_pass(jp):
                        g.wait_ge(sem["wst"].h, 16 * (store_idx[(pr[0], jp)] + 1))
                nm, _ = slot_sem[u]
                if use_wscr and p > wb_pass(j):
                    g.wait_ge(sem["wst"].h, 16 * (store_idx[(wb_pass(j), j)] + 1))
                    g.dma_start(buf_of(u)[:],
                                wscr_slice(j)).then_inc(sem[nm].h, 16)
                else:
                    g.dma_start(buf_of(u)[:],
                                wall_d[:, unit_cols(e, k):unit_cols(e, k) + UNIT]).then_inc(sem[nm].h, 16)
        EM("gpsimd", g_stream)
        for u in issue_list:
            nm, v = slot_sem[u]
            if sem[nm].n < v:
                sem[nm].n = v

        def unit_slot(p, e, k):
            return buf_of((p, e, k))

        def unit_wait(eng, p, e, k):
            nm, v = slot_sem[(p, e, k)]
            W(eng, sem[nm], v)

        def emit_wstores(p):
            js = [j for j in store_seq if wb_pass(j) == p]
            base = store_idx[(p, js[0])]

            def s_wstores(sy, _js=js, _p=p):
                for j in _js:
                    e_, k_ = divmod(j, NUNIT_E)
                    nm, v = slot_sem[(_p, e_, k_)]
                    sy.wait_ge(sem[nm].h, v)
                    sy.dma_start(wscr_slice(j),
                                 unit_slot(_p, e_, k_)[:]).then_inc(sem["wst"].h, 16)
            EM("sync", s_wstores)
            sem["wst"].inc(16 * len(js))

        stm, stm2, stvar, strst = (stat[:, i:i + 1] for i in range(4))
        nmx = stat[:, 4:5]
        ssum = stat[:, 5:6]
        rsum = stat[:, 6:7]
        stsq2 = stat[:, 7:8]

        for p in range(NPASS):
            for tc in range(TCP):
                ti = p * TCP + tc
                buf = ti % 2
                row = ti * 128

                if ti < 2:
                    W("sync", sem["ln1d"], 31 + ti)
                else:
                    W("sync", sem["yev"], ti - 1)
                W("sync", sem["dul"], 16 * (ti + 1))
                xt_tgt = sem["x2ld"].inc(32)

                def s_xld(sy, _r=row, _b=buf):
                    sy.dma_start(xt[_b][:], x_own_d[_r:_r + 128, :]).then_inc(sem["x2ld"].h, 16)
                    sy.dma_start(dut[_b][:], du_d[_r:_r + 128, :]).then_inc(sem["x2ld"].h, 16)
                EM("sync", s_xld)

                W("tensor", sem["abdone"], 1)
                if ti > 0:
                    W("tensor", sem["cev"], ti)
                if p > 0 and tc == 0:
                    W("tensor", sem["thd"], p * E * IC)

                def t_y(t, _ti=ti):
                    col = _ti * 128
                    for j in range(4):
                        t.matmul(pA[:, j * 512:(j + 1) * 512], xsbf[:, col:col + 128],
                                 c2re[:, j * 512:(j + 1) * 512], start=True, stop=False,
                                 skip_group_check=True)
                    for j in range(4):
                        i = t.matmul(pA[:, j * 512:(j + 1) * 512],
                                     xsbf[:, T + col:T + col + 128],
                                     c2im[:, j * 512:(j + 1) * 512], start=False, stop=True,
                                     skip_group_check=True)
                    i.then_inc(sem["ymm"].h, 1)
                EM("tensor", t_y)
                ymt = sem["ymm"].inc()

                W("vector", sem["x2ld"], xt_tgt)
                W("vector", sem["ymm"], ymt)
                if p > 0:
                    W("vector", sem["outst"], 16 * ((p - 1) * TCP + tc + 1))

                def v_x1(v, _b=buf, _tc=tc):
                    v = DrainAll(v)
                    v.tensor_tensor(x1[:, _tc, :], xt[_b][:], pA[:], ALU.add)
                    v.tensor_tensor(x1[:, _tc, :], x1[:, _tc, :], dut[_b][:], ALU.add).then_inc(sem["yev"].h, 1)
                EM("vector", v_x1)
                yevt = sem["yev"].inc()

                W("scalar", sem["yev"], yevt)

                def s_sq2(s, _tc=tc):
                    s.activation(xsqc, x1[:, _tc, :], AF.Square,
                                 accum_out=stsq2).then_inc(sem["sqd"].h, 1)
                EM("scalar", s_sq2)
                sq2 = sem["sqd"].inc()

                W("vector", sem["sqd"], sq2)

                def v_st2(v, _tc=tc):
                    v = DrainAll(v)
                    v.reduce_sum(stm[:], x1[:, _tc, :], axis=AX.X)
                    v.tensor_single_scalar(stm[:], stm[:], INV_D, ALU.mult)
                    v.tensor_tensor(stm2[:], stm[:], stm[:], ALU.mult)
                    v.tensor_single_scalar(stvar[:], stsq2[:], INV_D, ALU.mult)
                    v.tensor_tensor(stvar[:], stvar[:], stm2[:], ALU.subtract)
                    v.tensor_single_scalar(stvar[:], stvar[:], EPS, ALU.add).then_inc(sem["scand"].h, 1)
                EM("vector", v_st2)
                st2 = sem["scand"].inc()

                W("scalar", sem["scand"], st2)

                def s_rstd2(s):
                    s.activation(strst[:], stvar[:], AF.Ln)
                    s.drain()
                    s.activation(strst[:], strst[:], AF.Exp, scale=-0.5).then_inc(sem["rstd"].h, 1)
                EM("scalar", s_rstd2)
                rs2 = sem["rstd"].inc()

                W("vector", sem["rstd"], rs2)
                W("vector", sem["ctr"], ti)

                def v_ln2(v, _tc=tc):
                    v.tensor_scalar(lnc[:], x1[:, _tc, :], stm[:], strst[:],
                                    ALU.subtract, ALU.mult).then_inc(sem["scand"].h, 1)
                EM("vector", v_ln2)
                ln2r = sem["scand"].inc()

                W("tensor", sem["scand"], ln2r)
                W("tensor", sem["yev"], yevt)

                def t_ctr(t):
                    for c in range(HC):
                        i = t.transpose(pA[:, c * 128:(c + 1) * 128],
                                        lnc[:, c * 128:(c + 1) * 128], ident[:])
                    i.then_inc(sem["ctr"].h, 1)
                EM("tensor", t_ctr)
                ctrt = sem["ctr"].inc()

                W("vector", sem["ctr"], ctrt)

                def v_cev(v, _tc=tc):
                    for g4 in range(4):
                        src = pA[:, g4 * 512:(g4 + 1) * 512].rearrange("p (a b) -> p a b", a=4)
                        dst = ln2t[:, 4 * g4:4 * g4 + 4, _tc * 128:(_tc + 1) * 128]
                        i = v.tensor_copy(dst, src)
                    i.then_inc(sem["cev"].h, 1)
                EM("vector", v_cev)
                cevt = sem["cev"].inc()

                W("tensor", sem["cev"], cevt)
                if ti > 0:
                    W("tensor", sem["wtev"], ti)
                if p > 0 and tc == 0:
                    W("tensor", sem["dnev"], dnev_after(p - 1, E - 1, HQ - 1))
                    W("tensor", sem["wbev"], p * E)

                def t_r(t, _tc=tc):
                    for c in range(HC):
                        i = t.matmul(pB[0:8, 0:128], rwtb[:, c * 8:(c + 1) * 8],
                                     ln2t[:, c, _tc * 128:(_tc + 1) * 128],
                                     start=(c == 0), stop=(c == HC - 1))
                    i.then_inc(sem["rmm"].h, 1)
                EM("tensor", t_r)
                rmt = sem["rmm"].inc()

                W("vector", sem["rmm"], rmt)

                def v_rev(v):
                    v.tensor_copy(rfm[:], pB[0:8, 0:128]).then_inc(sem["rev"].h, 1)
                EM("vector", v_rev)
                revt = sem["rev"].inc()

                W("tensor", sem["rev"], revt)

                def t_lt(t):
                    t.transpose(pB[:, 512:520], rfm[:], ident[0:8, 0:8]).then_inc(sem["ltr"].h, 1)
                EM("tensor", t_lt)
                ltt = sem["ltr"].inc()

                W("vector", sem["ltr"], ltt)

                def v_sm1(v):
                    v = DrainAll(v)
                    v.tensor_copy(ltm[:], pB[:, 512:520])
                    v.reduce_max(nmx[:], ltm[:], axis=AX.X)
                    v.tensor_single_scalar(nmx[:], nmx[:], -1.0, ALU.mult).then_inc(sem["scand"].h, 1)
                EM("vector", v_sm1)
                sm1 = sem["scand"].inc()

                W("scalar", sem["scand"], sm1)

                def s_exp(s):
                    s.activation(prob[:], ltm[:], AF.Exp, bias=nmx[:]).then_inc(sv.h, 1)
                EM("scalar", s_exp)
                sxp = sv.inc()

                W("vector", sv, sxp)

                def v_top(v, _tc=tc):
                    v = DrainAll(v)
                    v.reduce_sum(ssum[:], prob[:], axis=AX.X)
                    v.reciprocal(rsum[:], ssum[:])
                    v.tensor_scalar(prob[:], prob[:], rsum[:], None, ALU.mult)
                    for e in range(E):
                        pc = prob[:, e:e + 1]
                        v.tensor_scalar(ptmp[:, e * 8:(e + 1) * 8], prob[:], pc, None, ALU.is_gt)
                        v.tensor_scalar(ptmp2[:, e * 8:(e + 1) * 8], prob[:], pc, None, ALU.is_equal)
                    v.tensor_tensor(ptmp2[:], ptmp2[:], ltbig[:], ALU.mult)
                    v.tensor_tensor(ptmp[:], ptmp[:], ptmp2[:], ALU.add)
                    for e in range(E):
                        v.tensor_tensor(ptmp[:, e * 8:(e + 1) * 8], ptmp[:, e * 8:(e + 1) * 8],
                                        prob[:], ALU.mult)
                    pt3 = ptmp[:].rearrange("p (a b) -> p a b", a=8)
                    v.tensor_reduce(pS[:], pt3, axis=AX.X, op=ALU.add)
                    v.tensor_single_scalar(pS[:], pS[:], 0.9, ALU.is_le)
                    v.tensor_tensor(wsb[:, _tc * 8:(_tc + 1) * 8], pS[:], prob[:],
                                    ALU.mult).then_inc(sem["topd"].h, 1)
                EM("vector", v_top)
                topt = sem["topd"].inc()

                W("tensor", sem["topd"], topt)

                def t_wt(t, _tc=tc):
                    t.transpose(pB[0:8, 1024:1152], wsb[:, _tc * 8:(_tc + 1) * 8],
                                ident[:]).then_inc(sem["wtr"].h, 1)
                EM("tensor", t_wt)
                wtrt = sem["wtr"].inc()

                W("vector", sem["wtr"], wtrt)

                def v_wtev(v, _tc=tc):
                    v.tensor_copy(wtt[:, _tc * 128:(_tc + 1) * 128],
                                  pB[0:8, 1024:1152]).then_inc(sem["wtev"].h, 1)
                EM("vector", v_wtev)
                sem["wtev"].inc()

            if debug and p == 0:
                W("sync", sem["wtev"], TCP)

                def s_dbg2(sy):
                    for tc_ in range(TCP):
                        sy.dma_start(dbg_x1_d[tc_ * 128:(tc_ + 1) * 128, :],
                                     x1[:, tc_, :]).then_inc(sem["dbgst"].h, 16)
                    sy.dma_start(dbg_w_d, wsb[:]).then_inc(sem["dbgst"].h, 16)
                    sy.dma_start(dbg_ln2_d, ln2t[:].rearrange("p a b -> p (a b)")).then_inc(sem["dbgst"].h, 16)
                EM("sync", s_dbg2)
                DBG2 = sem["dbgst"].inc(16 * 6)

            W("tensor", sem["wtev"], (p + 1) * TCP)
            if p == 0:
                W("tensor", sem["xld"], OH_LD)
            for e in range(E):
                W("tensor", sem["wbev"], p * E + e)

                def t_wb(t, _e=e):
                    t.matmul(pB[:, 0:512], oneh[:, _e * 128:(_e + 1) * 128], wtt[:],
                             start=True, stop=True).then_inc(sem["wbmm"].h, 1)
                EM("tensor", t_wb)
                wbt = sem["wbmm"].inc()

                W("vector", sem["wbmm"], wbt)

                def v_wb(v, _e=e):
                    i = v.tensor_copy(wball[:, _e, :], pB[:, 0:512])
                    if _e == E - 1:
                        v.drain()
                    i.then_inc(sem["wbev"].h, 1)
                EM("vector", v_wb)
                sem["wbev"].inc()

            if use_wscr and p <= 1:
                emit_wstores(p)

            for e in range(E):
                eg = p * E + e

                for ic in range(IC):
                    k = gu_iter(p, e, ic)
                    pair = ic % 2
                    gq = pA[:, pair * 1024:pair * 1024 + 512]
                    uq = pA[:, pair * 1024 + 512:pair * 1024 + 1024]
                    if e == 0 and ic < 2:
                        W("tensor", sem["cev"], (p + 1) * TCP)
                    W("tensor", sem["silud"], k - 1)
                    W("tensor", sem["thd"], k - 1)
                    unit_wait("tensor", p, e, 2 * ic)
                    unit_wait("tensor", p, e, 2 * ic + 1)

                    def t_gu(t, _gq=gq, _uq=uq,
                             _gb=unit_slot(p, e, 2 * ic), _ub=unit_slot(p, e, 2 * ic + 1)):
                        for c in range(HC):
                            i = t.matmul(_gq, _gb[:, c * 128:(c + 1) * 128],
                                         ln2t[:, c, :], start=(c == 0), stop=(c == HC - 1))
                        i.then_inc(sem["gstop"].h, 1)
                        for c in range(HC):
                            i = t.matmul(_uq, _ub[:, c * 128:(c + 1) * 128],
                                         ln2t[:, c, :], start=(c == 0), stop=(c == HC - 1))
                        i.then_inc(sem["ustop"].h, 1)
                    EM("tensor", t_gu)
                    gst = sem["gstop"].inc()
                    ust = sem["ustop"].inc()

                    W("scalar", sem["gstop"], gst)
                    W("scalar", sem["thd"], k - 1)

                    def s_silu(s, _gq=gq, _pair=pair):
                        s.activation(sg[_pair][:], _gq, AF.Silu).then_inc(sem["silud"].h, 1)
                    EM("scalar", s_silu)
                    sil = sem["silud"].inc()

                    W("vector", sem["ustop"], ust)
                    W("vector", sem["silud"], sil)

                    def v_hid(v, _uq=uq, _pair=pair, _ic=ic, _e=e):
                        v.tensor_tensor(th[:], sg[_pair][:], _uq, ALU.mult)
                        v.drain()
                        v.tensor_tensor(hid[:, _ic, :], th[:], wball[:, _e, :],
                                        ALU.mult).then_inc(sem["thd"].h, 1)
                    EM("vector", v_hid)
                    sem["thd"].inc()

                for hq in range(HQ):
                    for ic in range(IC):
                        kd = GU_UNITS + hq * 8 + ic // 4
                        if ic % 4 == 0:
                            unit_wait("tensor", p, e, kd)
                        W("tensor", sem["thd"], eg * IC + ic + 1)

                        def t_dn(t, _ic=ic, _hq=hq, _eg=eg, _db=unit_slot(p, e, kd)):
                            dbuf = _db
                            loc = (_ic % 4) * 512
                            for tc in range(TCP):
                                if _ic == 0:
                                    prev = (_eg * HQ + _hq - 1) * TCP + tc + 1
                                    if prev > 0:
                                        t.wait_ge(sem["dnev"].h, prev)
                                i = t.matmul(pB[:, tc * 512:(tc + 1) * 512],
                                             hid[:, _ic, tc * 128:(tc + 1) * 128],
                                             dbuf[:, loc:loc + 512],
                                             start=(_ic == 0), stop=(_ic == IC - 1),
                                             skip_group_check=True)
                                if _ic == IC - 1:
                                    i.then_inc(sem["dnstop"].h, 1)
                                elif _ic % 4 == 3 and tc == TCP - 1:
                                    i.then_inc(sem["dnu"].h, 1)
                        EM("tensor", t_dn)
                    dnst = sem["dnstop"].inc(TCP)

                    for tc in range(TCP):
                        W("vector", sem["dnstop"], dnst - TCP + tc + 1)
                        if debug and p == 0 and e == 0 and hq == 0 and tc == 0:
                            W("vector", sem["dbgst"], DBG2)

                        def v_dn(v, _tc=tc, _hq=hq):
                            v.tensor_tensor(x1[:, _tc, _hq * 512:(_hq + 1) * 512],
                                            x1[:, _tc, _hq * 512:(_hq + 1) * 512],
                                            pB[:, _tc * 512:(_tc + 1) * 512],
                                            ALU.add).then_inc(sem["dnev"].h, 1)
                        EM("vector", v_dn)
                        sem["dnev"].inc()

            for tc in range(TCP):
                W("sync", sem["dnev"], dnev_after(p, E - 1, HQ - 1))

                def s_out(sy, _p=p, _tc=tc):
                    sy.dma_start(out_d[_p * 512 + _tc * 128:_p * 512 + (_tc + 1) * 128, :],
                                 x1[:, _tc, :]).then_inc(sem["outst"].h, 16)
                EM("sync", s_out)
                sem["outst"].inc(16)

        W("sync", sem["outst"], sem["outst"].n)

        with nc.Block() as blk2:
            P2.replay(blk2)

    return nc


_CACHE = {}


def _prep_shared(inputs):
    gw = np.asarray(inputs["L0_ffn_gate_w"], dtype=np.float32)
    uw = np.asarray(inputs["L0_ffn_up_w"], dtype=np.float32)
    dw = np.asarray(inputs["L0_ffn_down_w"], dtype=np.float32)
    wall = np.empty((128, E * ECOLS), dtype=np.float32)
    for e in range(E):
        base = e * ECOLS
        g = gw[e].reshape(IC, 128, HC, 128).transpose(3, 0, 2, 1).reshape(128, 65536)
        u = uw[e].reshape(IC, 128, HC, 128).transpose(3, 0, 2, 1).reshape(128, 65536)
        d = dw[e].reshape(HQ, 512, IC, 128).transpose(3, 0, 2, 1).reshape(128, 65536)
        wall[:, base:base + 65536] = g
        wall[:, base + 65536:base + 2 * 65536] = u
        wall[:, base + 2 * 65536:base + 3 * 65536] = d
    rw = np.asarray(inputs["L0_ffn_router_w"], dtype=np.float32)
    rwt = np.ascontiguousarray(rw.T.reshape(HC, 128, E).transpose(1, 0, 2).reshape(128, 128))
    return {
        "lam_ur": np.ascontiguousarray(np.asarray(
            inputs["L0_attn_Lambda_unconstrained_re"], np.float32).reshape(NH, 1)),
        "lam_im": np.ascontiguousarray(np.asarray(
            inputs["L0_attn_Lambda_im"], np.float32).reshape(NH, 1)),
        "logd": np.ascontiguousarray(np.asarray(
            inputs["L0_attn_log_Delta"], np.float32)[:NH].reshape(NH, 1)),
        "dvec": np.ascontiguousarray(np.asarray(
            inputs["L0_attn_D"], np.float32).reshape(1, D)),
        "bre": np.ascontiguousarray(np.asarray(inputs["L0_attn_B_re"], np.float32)),
        "bim": np.ascontiguousarray(np.asarray(inputs["L0_attn_B_im"], np.float32)),
        "cret": np.ascontiguousarray(np.asarray(inputs["L0_attn_C_re"], np.float32).T),
        "cimt": np.ascontiguousarray(np.asarray(inputs["L0_attn_C_im"], np.float32).T),
        "rwt": rwt,
        "wall": wall,
    }


def kernel(**inputs):
    u = np.asarray(inputs["L0_attn_u"], dtype=np.float32)
    shared = _prep_shared(inputs)
    if "nc" not in _CACHE:
        _CACHE["nc"] = build_program(use_wscr=True)
    nc = _CACHE["nc"]

    in_maps = []
    for c in range(NCORE):
        b, h = c // 2, c % 2
        x_own = np.ascontiguousarray(u[b, h * T:(h + 1) * T, :])
        x_prev = np.ascontiguousarray(u[b, 0:T, :]) if h == 1 else x_own
        m = dict(shared)
        m["x_own"] = x_own
        m["x_prev"] = x_prev
        m["cmask"] = np.full((NH, 1), float(h), dtype=np.float32)
        in_maps.append(m)

    res = run_bass_kernel_spmd(nc, in_maps, core_ids=list(range(NCORE)))
    out = np.empty((B, L, D), dtype=np.float32)
    for c in range(NCORE):
        b, h = c // 2, c % 2
        out[b, h * T:(h + 1) * T, :] = res.results[c]["out"]
    return out
